# version 09
# baseline (speedup 1.0000x reference)
import numpy as np
import ml_dtypes
DEBUG_BOUNDS = False
import concourse.bass as bass
import concourse.mybir as mybir
import bass_rust
from concourse.alu_op_type import AluOpType
from concourse.bass_utils import run_bass_kernel_spmd

F32 = mybir.dt.float32
FP16 = mybir.dt.float16
BF16 = mybir.dt.bfloat16
AF = mybir.ActivationFunctionType
AX = mybir.AxisListType

D = 1024
T = 2048
NH = 16
HD = 64
DL = 256
DH = 4096
NCH = 8
NTC = 4
NKT = 16
NEG = -1.0e30
EXPSHIFT = 56.0


def _vp(pairs):
    return bass_rust.VecI64Pair([list(x) for x in pairs])


def _setap(ap, pairs, offset=None):
    ap = ap.copy()
    ap.ap = _vp(pairs)
    if offset is not None:
        ap.offset = offset
    return ap


class Plan:

    ENGINES = ("sync", "gpsimd", "tensor", "vector", "scalar")
    QDEPTH = 8
    ESEM = {"tensor": "tS", "vector": "vS", "scalar": "sS"}

    def __init__(self, nc):
        self.nc = nc
        self.ops = {e: [] for e in self.ENGINES}
        self.sems = {}
        self.cnt = {}
        for name in ("tS", "vS", "sS"):
            self.sems[name] = nc.alloc_semaphore(name)
            self.cnt[name] = 0
        for q in ("sync", "gpsimd"):
            for i in range(self.QDEPTH):
                name = f"q_{q}_{i}"
                self.sems[name] = nc.alloc_semaphore(name)
                self.cnt[name] = 0
        self.qstate = {"sync": [0, False], "gpsimd": [0, False]}
        self.release = {f"q_{q}_{i}": [] for q in ("sync", "gpsimd")
                        for i in range(self.QDEPTH)}

    def emit(self, eng, fn):
        self.ops[eng].append(fn)

    def wait(self, eng, sem, val):
        if val <= 0:
            return
        if sem.startswith("q_") and DEBUG_BOUNDS:
            import traceback
            if val not in self.bounds.get(sem, set()) and val != self.cnt[sem]:
                print("NON-BOUNDARY WAIT", sem, val, "cnt", self.cnt[sem])
                traceback.print_stack()
        s = self.sems[sem]
        self.emit(eng, lambda e, s=s, val=val: e.wait_ge(s, val))

    def wait_now(self, eng, sem):
        self.wait(eng, sem, self.cnt[sem])

    def _qsem(self, qeng):
        return f"q_{qeng}_{self.qstate[qeng][0]}"

    def dma(self, qeng, out, in_, **kw):
        st = self.qstate[qeng]
        if not st[1]:
            st[0] = (st[0] + 1) % self.QDEPTH
            st[1] = True
            sem0 = self._qsem(qeng)
            rel = self.release[sem0]
            if any(r is None for r in rel):
                for es in ("tS", "vS", "sS"):
                    self.wait_now(qeng, es)
            else:
                full = False
                for es, target in rel:
                    if self.cnt[es] >= target:
                        self.wait(qeng, es, target)
                    else:
                        full = True
                if full:
                    for es in ("tS", "vS", "sS"):
                        self.wait_now(qeng, es)
            self.release[sem0] = []
        sem = self._qsem(qeng)
        s = self.sems[sem]
        self.cnt[sem] += 16
        self.emit(qeng, lambda e, o=out, i=in_, kw=kw, s=s: e.dma_start(out=o, in_=i, **kw).then_inc(s, 16))
        return sem, self.cnt[sem]

    def qwait(self, eng, qeng):
        st = self.qstate[qeng]
        sem = self._qsem(qeng)
        st[1] = False
        self.wait(eng, sem, self.cnt[sem])
        es = self.ESEM.get(eng)
        self.release[sem].append((es, self.cnt[es] + 1) if es else None)

    def qbarrier(self, eng, qeng):
        self.qstate[qeng][1] = False
        es = self.ESEM.get(eng)
        for i in range(self.QDEPTH):
            sem = f"q_{qeng}_{i}"
            if self.cnt[sem] > 0:
                self.wait(eng, sem, self.cnt[sem])
                self.release[sem].append((es, self.cnt[es] + 1) if es else None)

    def qclose(self, qeng):
        sem = self._qsem(qeng)
        self.qstate[qeng][1] = False
        return (sem, self.cnt[sem])

    def qwait_tok(self, eng, tok):
        sem, val = tok
        if val <= 0:
            return
        if self.cnt[sem] > val:
            for q, st in self.qstate.items():
                if f"q_{q}_{st[0]}" == sem:
                    st[1] = False
            val = self.cnt[sem]
        self.wait(eng, sem, val)
        es = self.ESEM.get(eng)
        self.release[sem].append((es, self.cnt[es] + 1) if es else None)

    def op(self, eng, method, *args, **kw):
        sem = {"tensor": "tS", "vector": "vS", "scalar": "sS"}[eng]
        s = self.sems[sem]
        self.cnt[sem] += 1
        def fn(e, method=method, args=args, kw=kw, s=s):
            getattr(e, method)(*args, **kw).then_inc(s, 1)
        self.emit(eng, fn)
        return self.cnt[sem]

    def finalize(self):
        if hasattr(self, "phases"):
            self.nc._phases = list(self.phases) + [("end", self.cnt["tS"])]
        for q in ("sync", "gpsimd"):
            for i in range(self.QDEPTH):
                sem = f"q_{q}_{i}"
                if self.cnt[sem] > 0:
                    self.wait(q, sem, self.cnt[sem])
        with self.nc.Block() as block:
            for eng in self.ENGINES:
                ops = self.ops[eng]
                def make_body(ops):
                    def body(e):
                        for fn in ops:
                            fn(e)
                    return body
                getattr(block, eng)(make_body(ops))


def build_nc(dbg=(), upto=99):
    dbg = set(dbg)
    nc = bass.Bass("TRN2", target_bir_lowering=False)

    di = {}
    def dram_in(name, shape, dt=F32):
        di[name] = nc.dram_tensor(name, list(shape), dt, kind="ExternalInput").ap()
        return di[name]

    x_in = dram_in("x", (T, D))
    w_dkv = dram_in("w_dkv", (D, DL))
    w_uk = dram_in("w_uk", (DL, D))
    w_uv = dram_in("w_uv", (DL, D))
    w_uq = dram_in("w_uq", (D, D))
    w_o = dram_in("w_o", (D, D))
    w1t = dram_in("w1t", (D, DH))
    w2t = dram_in("w2t", (D, DH))
    h1t = dram_in("h1t", (DH, DH))
    h2t = dram_in("h2t", (DH, DH))
    w3t = dram_in("w3t", (DH, D))
    svec_in = dram_in("svec", (128, 192))
    wqkvt = dram_in("wqkvt", (D, 3 * D))
    woutt = dram_in("woutt", (D, D))
    wposT = dram_in("wposT", (4, NH))
    coeffsT = dram_in("coeffsT", (101, 4))
    ctrl = dram_in("ctrl", (D, 64))
    f1m = dram_in("f1m", (D, DH))
    f2m = dram_in("f2m", (DH, D))
    identf = dram_in("identf", (128, 128))
    identb = dram_in("identb", (128, 128), BF16)
    onesb = dram_in("onesb", (1, 128), BF16)
    protod_in = dram_in("protod", (101, T))
    protod_lo_in = dram_in("protod_lo", (101, T))
    cmask_in = dram_in("cmask", (128, 4 * 512), BF16)
    tie_in = dram_in("tie", (1, T), BF16)

    out_hbm = nc.dram_tensor("out", [T, D], F32, kind="ExternalOutput").ap()

    dbg_t = {}
    def dbg_out(name, shape, dt=FP16):
        dbg_t[name] = nc.dram_tensor("dbg_" + name, list(shape), dt,
                                     kind="ExternalOutput").ap()
        return dbg_t[name]
    if "xn0" in dbg:
        dbg_out("xn0", (D, 2 * T))
    if "ckv" in dbg:
        dbg_out("ckv", (DL, 2 * T))
    if "kvq" in dbg:
        for n in ("khi", "klo", "qhi", "qlo"):
            dbg_out(n, (D, T))
        for n in ("vhi", "vlo"):
            dbg_out(n, (T, D))
    if "affn" in dbg:
        dbg_out("ahi", (DH, T)); dbg_out("alo", (DH, T))
    if "ctx0" in dbg:
        dbg_out("ctxhi", (D, T)); dbg_out("ctxlo", (D, T))
    if "x1a" in dbg:
        dbg_out("x1a", (D, T), F32)
    if "x1" in dbg:
        dbg_out("x1", (D, T), F32)
    if "gsh" in dbg:
        dbg_out("gsh", (16, 4224))
    if "xn1" in dbg:
        dbg_out("xn1", (D, 2 * T))
    if "kvq1" in dbg:
        for n in ("khi", "klo", "qhi", "qlo"):
            dbg_out(n, (D, T))
        for n in ("vhi", "vlo"):
            dbg_out(n, (T, D))
    if "perm" in dbg:
        dbg_out("perm", (64, T))
        dbg_out("lg", (64, T), F32)
    if "ym" in dbg:
        dbg_out("ymhi", (DH, T)); dbg_out("ymlo", (DH, T))
    if "x2a" in dbg:
        dbg_out("x2a", (D, T), F32)
    if "vslab" in dbg:
        dbg_out("vslabh", (128, 4096))
        dbg_out("vslabl", (128, 4096))

    def scratch(name, shape, dt=FP16):
        return nc.dram_tensor(name, list(shape), dt, kind="Internal").ap()

    s_khi = scratch("s_khi", (D, T)); s_klo = scratch("s_klo", (D, T))
    s_qhi = scratch("s_qhi", (D, T)); s_qlo = scratch("s_qlo", (D, T))
    s_vhi = scratch("s_vhi", (T, D)); s_vlo = scratch("s_vlo", (T, D))
    s_ahi = scratch("s_ahi", (DH, T)); s_alo = scratch("s_alo", (DH, T))
    s_bhi = scratch("s_bhi", (DH, T)); s_blo = scratch("s_blo", (DH, T))
    s_ctxhi = scratch("s_ctxhi", (D, T)); s_ctxlo = scratch("s_ctxlo", (D, T))
    s_h1 = scratch("s_h1", (DH, DH)); s_h2 = scratch("s_h2", (DH, DH))
    s_w3 = scratch("s_w3", (DH, D))
    s_f2hi = scratch("s_f2hi", (DH, D)); s_f2lo = scratch("s_f2lo", (DH, D))
    s_gsh = [scratch(f"s_gsh{i}", (128 * 4224,), FP16) for i in range(2)]

    X = [nc.alloc_sbuf_tensor(f"X{c}", [128, T], F32) for c in range(NCH)]
    PA = nc.alloc_sbuf_tensor("PA", [128, 32768], FP16)
    CKV = nc.alloc_sbuf_tensor("CKV", [128, 8192], FP16)
    WS32 = nc.alloc_sbuf_tensor("WS32", [128, 2048], F32)
    WH = nc.alloc_sbuf_tensor("WH", [128, 4096], FP16)
    WL = nc.alloc_sbuf_tensor("WL", [128, 4096], FP16)
    SC32 = nc.alloc_sbuf_tensor("SC32", [128, 2048], F32)
    MISC = nc.alloc_sbuf_tensor("MISC", [128, 2048], F32)
    GSH1 = nc.alloc_sbuf_tensor("GSH1", [128, 4224], FP16)
    IDF = nc.alloc_sbuf_tensor("IDF", [128, 128], F32)
    IDB = nc.alloc_sbuf_tensor("IDB", [128, 128], BF16)
    IDH = nc.alloc_sbuf_tensor("IDH", [128, 128], FP16)
    ONE1 = nc.alloc_sbuf_tensor("ONE1", [1, 128], BF16)
    ONESC = nc.alloc_sbuf_tensor("ONESC", [128, 1], F32)
    EPS = nc.alloc_sbuf_tensor("EPS", [128, 1], F32)
    CMSK = nc.alloc_sbuf_tensor("CMSK", [128, 2048], BF16)
    SVEC = nc.alloc_sbuf_tensor("SVEC", [128, 192], F32)
    PS = nc.alloc_psum_tensor("PS", [128, 4096], F32)

    p = Plan(nc)
    p.phases = []
    def mark(name):
        p.phases.append((name, p.cnt["tS"]))

    def bank(i, r0=0, r1=128, w=512):
        return PS.ap()[r0:r1, i * 512:i * 512 + w]

    def pa(off, w, r0=0, r1=128):
        return PA.ap()[r0:r1, off:off + w]

    def ckv_ap(off, w, r0=0, r1=128):
        return CKV.ap()[r0:r1, off:off + w]

    def mm(bk, lhsT, rhs, start, stop, tp=None):
        return p.op("tensor", "matmul", bk, lhsT, rhs, start=start, stop=stop,
                    tile_position=tp)

    def bcast_row(dst_ap, src_row_ap, width, q="gpsimd"):
        pitch = src_row_ap.ap[0][0]
        src = _setap(src_row_ap, [[pitch, 1], [0, dst_ap.shape[0]], [1, width]])
        return p.dma(q, dst_ap, src)

    p.dma("sync", IDF.ap(), identf[:, :])
    p.dma("sync", IDB.ap(), identb[:, :])
    p.dma("sync", ONE1.ap(), onesb[:, :])
    p.dma("sync", CMSK.ap(), cmask_in[:, :])
    p.dma("sync", SVEC.ap(), svec_in[:, :])
    p.qwait("vector", "sync")
    v_ones = p.op("vector", "memset", ONESC.ap(), 1.0)
    p.op("vector", "memset", EPS.ap(), 1e-5)
    p.op("vector", "tensor_copy", IDH.ap(), IDF.ap())

    mark("xload")
    ld_prev = 0
    evac_prev = [0, 0]
    for t in range(NKT):
        par = t & 1
        stg = SC32.ap()[:, par * 1024:par * 1024 + 1024]
        if t >= 2:
            p.wait("sync", "vS", evac_prev[par])
        p.dma("sync", stg, x_in[t * 128:(t + 1) * 128, :])
        p.qwait("tensor", "sync")
        if t >= 2:
            p.wait("tensor", "vS", evac_prev[par])
        tcnt = 0
        for c in range(NCH):
            tcnt = p.op("tensor", "transpose", PS.ap()[:, par * 1024 + c * 128: par * 1024 + c * 128 + 128],
                        stg[:, c * 128:(c + 1) * 128], IDF.ap())
        p.wait("vector", "tS", tcnt)
        for c in range(NCH):
            ev = p.op("vector", "tensor_copy", X[c].ap()[:, t * 128:(t + 1) * 128],
                      PS.ap()[:, par * 1024 + c * 128: par * 1024 + c * 128 + 128])
        evac_prev[par] = ev

    def ln_phase(tag):
        sq_cnt = 0
        mm_cnt = 0
        for c in range(NCH):
            p.wait("scalar", "tS", mm_cnt)
            p.wait("scalar", "vS", p.cnt["vS"])
            sq_cnt = p.op("scalar", "activation", SC32.ap(), X[c].ap(), AF.Square,
                          scale=1.0)
            p.wait("tensor", "sS", sq_cnt)
            for tc in range(NTC):
                mm(bank(tc, 0, 1), ONESC.ap(), X[c].ap()[:, tc * 512:(tc + 1) * 512],
                   start=(c == 0), stop=(c == NCH - 1))
                mm_cnt = mm(bank(4 + tc, 0, 1), ONESC.ap(), SC32.ap()[:, tc * 512:(tc + 1) * 512],
                            start=(c == 0), stop=(c == NCH - 1))
        p.wait("vector", "tS", mm_cnt)
        p.wait_now("vector", "sS")
        for tc in range(NTC):
            p.op("vector", "tensor_scalar", MISC.ap()[0:1, tc * 512:(tc + 1) * 512],
                 bank(tc, 0, 1), 1.0 / D, None, op0=AluOpType.mult)
            p.op("vector", "tensor_scalar", SC32.ap()[0:1, tc * 512:(tc + 1) * 512],
                 bank(4 + tc, 0, 1), 1.0 / D, None, op0=AluOpType.mult)
        p.wait_now("vector", "vS")
        p.op("vector", "scalar_tensor_tensor", WS32.ap()[0:1, 0:2048], MISC.ap()[0:1, :],
             1.0, MISC.ap()[0:1, :], op0=AluOpType.mult, op1=AluOpType.mult)
        p.wait_now("vector", "vS")
        vcnt = p.op("vector", "tensor_tensor", PS.ap()[0:1, 0:2048], SC32.ap()[0:1, :],
                    WS32.ap()[0:1, 0:2048], op=AluOpType.subtract)
        p.wait("scalar", "vS", vcnt)
        scnt = p.op("scalar", "activation", PS.ap()[0:1, 2048:4096], PS.ap()[0:1, 0:2048],
                    AF.Sqrt, bias=EPS.ap()[0:1, :], scale=1.0)
        p.wait("vector", "sS", scnt)
        vcnt2 = p.op("vector", "reciprocal", MISC.ap()[32:33, 0:2048], PS.ap()[0:1, 2048:4096])
        p.wait("gpsimd", "vS", vcnt2)
        p.wait_now("gpsimd", "sS")
        c_mb = bcast_row(WS32.ap()[:, 0:2048], MISC.ap()[0:1, 0:2048], 2048)
        c_rb = bcast_row(SC32.ap()[:, 1024:2048], MISC.ap()[32:33, 0:1024], 1024)
        c_rb2 = bcast_row(SC32.ap()[:, 0:1024], MISC.ap()[32:33, 1024:2048], 1024)
        p.qwait("vector", "gpsimd")
        for c in range(NCH):
            for h in range(2):
                xsl = X[c].ap()[:, h * 1024:(h + 1) * 1024]
                msl = WS32.ap()[:, h * 1024:(h + 1) * 1024]
                rsl = SC32.ap()[:, 1024:2048] if h == 0 else SC32.ap()[:, 0:1024]
                t1 = PS.ap()[:, 0:1024] if (c & 1) == 0 else PS.ap()[:, 1024:2048]
                xn32 = PS.ap()[:, 2048:3072] if (c & 1) == 0 else PS.ap()[:, 3072:4096]
                p.wait_now("vector", "vS")
                p.op("vector", "tensor_tensor", t1, xsl, msl, op=AluOpType.subtract)
                p.wait_now("vector", "vS")
                vc = p.op("vector", "tensor_tensor", xn32, t1, rsl, op=AluOpType.mult)
                p.wait("scalar", "vS", vc)
                sc = p.op("scalar", "copy", pa(c * 2048 + h * 1024, 1024), xn32)
                p.wait("vector", "sS", sc)
                p.op("vector", "scalar_tensor_tensor",
                     pa(16384 + c * 2048 + h * 1024, 1024), xn32, 1.0,
                     pa(c * 2048 + h * 1024, 1024),
                     op0=AluOpType.mult, op1=AluOpType.subtract)
        return

    def stage_general(wsrc, rows, col0, ncols, slot, row0=0):
        nk = rows // 128
        w = nk * ncols
        wtot = wsrc.shape[1]
        src = _setap(wsrc, [[wtot, 128], [128 * wtot, nk], [1, ncols]],
                     offset=row0 * wtot + col0)
        p.wait_now("sync", "sS")
        p.wait_now("sync", "vS")
        p.dma("sync", WS32.ap()[:, 0:w], src)
        p.qwait("scalar", "sync")
        p.wait_now("scalar", "tS")
        p.wait_now("scalar", "vS")
        p.qbarrier("scalar", "gpsimd")
        scnt = p.op("scalar", "copy", WH.ap()[:, slot:slot + w], WS32.ap()[:, 0:w])
        p.wait("vector", "sS", scnt)
        p.qbarrier("vector", "gpsimd")
        vcnt = p.op("vector", "scalar_tensor_tensor", WL.ap()[:, slot:slot + w],
                    WS32.ap()[:, 0:w], 1.0, WH.ap()[:, slot:slot + w],
                    op0=AluOpType.mult, op1=AluOpType.subtract)
        return scnt, vcnt

    def stage_ternary_f32(wsrc, rows, col0, ncols, slot):
        nk = rows // 128
        w = nk * ncols
        wtot = wsrc.shape[1]
        src = _setap(wsrc, [[wtot, 128], [128 * wtot, nk], [1, ncols]], offset=col0)
        p.wait_now("sync", "sS")
        p.dma("sync", WS32.ap()[:, 0:w], src)
        p.qwait("scalar", "sync")
        p.wait_now("scalar", "tS")
        p.wait_now("scalar", "vS")
        p.qbarrier("scalar", "gpsimd")
        scnt = p.op("scalar", "copy", WH.ap()[:, slot:slot + w], WS32.ap()[:, 0:w])
        return scnt

    class Stager:

        def __init__(self, halfw=1024):
            self.halfw = halfw
            self.tdone = [p.cnt["tS"], p.cnt["tS"]]
            self.ws = [(p.cnt["sS"], p.cnt["vS"])] * 2
            self.i = 0
            self.par = 0

        def stage(self, wsrc, rows, col0, ncols, row0=0, ternary=False):
            par = self.i & 1
            self.par = par
            self.i += 1
            slot = par * self.halfw
            nk = rows // 128
            w = nk * ncols
            wtot = wsrc.shape[1]
            srcap = _setap(wsrc, [[wtot, 128], [128 * wtot, nk], [1, ncols]],
                           offset=row0 * wtot + col0)
            ps, pv = self.ws[par]
            p.wait("sync", "sS", ps)
            p.wait("sync", "vS", pv)
            p.dma("sync", WS32.ap()[:, par * 1024:par * 1024 + w], srcap)
            p.qwait("scalar", "sync")
            p.wait("scalar", "tS", self.tdone[par])
            scnt = p.op("scalar", "copy", WH.ap()[:, slot:slot + w],
                        WS32.ap()[:, par * 1024:par * 1024 + w])
            vcnt = 0
            if not ternary:
                p.wait("vector", "sS", scnt)
                p.wait("vector", "tS", self.tdone[par])
                vcnt = p.op("vector", "scalar_tensor_tensor", WL.ap()[:, slot:slot + w],
                            WS32.ap()[:, par * 1024:par * 1024 + w], 1.0,
                            WH.ap()[:, slot:slot + w],
                            op0=AluOpType.mult, op1=AluOpType.subtract)
            self.ws[par] = (scnt, vcnt)
            p.wait("tensor", "sS", scnt)
            if not ternary:
                p.wait("tensor", "vS", vcnt)
            return slot

        def done(self, mmcount):
            self.tdone[self.par] = mmcount

    class SpillSlots:

        def __init__(self, hi_buf, lo_buf, offs, rows=128):
            self.hi_buf, self.lo_buf = hi_buf, lo_buf
            self.offs = offs
            self.tok = [None] * len(offs)
            self.i = 0
            self.rows = rows

        def spill(self, src32, hi_dst, lo_dst):
            par = self.i % len(self.offs)
            self.i += 1
            ho, lo_ = self.offs[par]
            hi_st = self.hi_buf.ap()[0:self.rows, ho:ho + 512]
            lo_st = self.lo_buf.ap()[0:self.rows, lo_:lo_ + 512]
            if self.tok[par] is not None:
                p.qwait_tok("scalar", self.tok[par])
                p.qwait_tok("vector", self.tok[par])
            else:
                p.qbarrier("scalar", "gpsimd")
                p.qbarrier("vector", "gpsimd")
            p.wait_now("scalar", "sS")
            sc2 = p.op("scalar", "copy", hi_st, src32)
            p.wait("vector", "sS", sc2)
            vc2 = p.op("vector", "scalar_tensor_tensor", lo_st, src32, 1.0, hi_st,
                       op0=AluOpType.mult, op1=AluOpType.subtract)
            p.wait("gpsimd", "sS", sc2)
            p.wait("gpsimd", "vS", vc2)
            p.dma("gpsimd", hi_dst, hi_st)
            p.dma("gpsimd", lo_dst, lo_st)
            self.tok[par] = p.qclose("gpsimd")
            return sc2, vc2

    def xn_hi(c, tc=None, w=512):
        if tc is None:
            return pa(c * 2048, 2048)
        return pa(c * 2048 + tc * 512, w)

    def xn_lo(c, tc=None, w=512):
        if tc is None:
            return pa(16384 + c * 2048, 2048)
        return pa(16384 + c * 2048 + tc * 512, w)

    mark("ln0")
    ln_phase("ln0")
    if "xn0" in dbg:
        p.wait_now("gpsimd", "vS")
        p.wait_now("gpsimd", "sS")
        for c in range(NCH):
            p.dma("gpsimd", dbg_t["xn0"][c * 128:(c + 1) * 128, 0:T], xn_hi(c))
            p.dma("gpsimd", dbg_t["xn0"][c * 128:(c + 1) * 128, T:2 * T], xn_lo(c))
    if upto <= 0:
        p.finalize()
        return nc, di, dbg_t

    mark("ckv")
    for oc in range(2):
        sc, vc = stage_general(w_dkv, D, oc * 128, 128, slot=0)
        p.wait("tensor", "sS", sc)
        p.wait("tensor", "vS", vc)
        mmc = 0
        for pas in range(3):
            for kc in range(NCH):
                st = WH.ap()[:, kc * 128:(kc + 1) * 128] if pas < 2 else WL.ap()[:, kc * 128:(kc + 1) * 128]
                for tc in range(NTC):
                    mv = xn_hi(kc, tc) if pas != 1 else xn_lo(kc, tc)
                    mmc = mm(bank(tc), st, mv, start=(pas == 0 and kc == 0),
                             stop=(pas == 2 and kc == NCH - 1))
        p.wait("scalar", "tS", mmc)
        for tc in range(NTC):
            sc2 = p.op("scalar", "copy", ckv_ap(oc * 2048 + tc * 512, 512), bank(tc))
            p.wait("vector", "sS", sc2)
            p.op("vector", "scalar_tensor_tensor", ckv_ap(4096 + oc * 2048 + tc * 512, 512),
                 bank(tc), 1.0, ckv_ap(oc * 2048 + tc * 512, 512),
                 op0=AluOpType.mult, op1=AluOpType.subtract)
        p.wait_now("tensor", "vS")
    if "ckv" in dbg:
        p.wait_now("gpsimd", "vS")
        p.wait_now("gpsimd", "sS")
        for oc in range(2):
            p.dma("gpsimd", dbg_t["ckv"][oc * 128:(oc + 1) * 128, 0:T], ckv_ap(oc * 2048, 2048))
            p.dma("gpsimd", dbg_t["ckv"][oc * 128:(oc + 1) * 128, T:2 * T], ckv_ap(4096 + oc * 2048, 2048))
    if upto <= 1:
        p.finalize()
        return nc, di, dbg_t

    def ckv_hi(kc, tc=None, w=512):
        if tc is None:
            return ckv_ap(kc * 2048, 2048)
        return ckv_ap(kc * 2048 + tc * 512, w)

    def ckv_lo(kc, tc=None, w=512):
        if tc is None:
            return ckv_ap(4096 + kc * 2048, 2048)
        return ckv_ap(4096 + kc * 2048 + tc * 512, w)

    def spill_split(bk_ap, hi_dst, lo_dst, sc_off):
        hi_t = WH.ap()[:, 2048 + sc_off:2048 + sc_off + 512]
        lo_t = WL.ap()[:, 2048 + sc_off:2048 + sc_off + 512]
        sc2 = p.op("scalar", "copy", hi_t, bk_ap)
        p.wait("vector", "sS", sc2)
        vc2 = p.op("vector", "scalar_tensor_tensor", lo_t, bk_ap, 1.0, hi_t,
                   op0=AluOpType.mult, op1=AluOpType.subtract)
        p.wait("gpsimd", "sS", sc2)
        p.wait("gpsimd", "vS", vc2)
        p.dma("gpsimd", hi_dst, hi_t)
        return p.dma("gpsimd", lo_dst, lo_t)

    def proj_general(wsrc, d_out, mov_hi, mov_lo, nkc, spill_to, qkv_route=None):
        stg = Stager()
        spl = SpillSlots(WH, WL, [(2048, 2048), (2560, 2560), (3072, 3072), (3584, 3584)])
        epb = {0: (0, 0), 1: (0, 0)}
        for oc in range(d_out // 128):
            slot = stg.stage(wsrc, nkc * 128, oc * 128, 128)
            bpar = oc & 1
            p.wait("tensor", "sS", epb[bpar][0])
            p.wait("tensor", "vS", epb[bpar][1])
            mmc = 0
            for pas in range(3):
                for kc in range(nkc):
                    st = (WH if pas < 2 else WL).ap()[:, slot + kc * 128:slot + kc * 128 + 128]
                    for tc in range(NTC):
                        mv = mov_hi(kc, tc) if pas != 1 else mov_lo(kc, tc)
                        mmc = mm(bank(bpar * 4 + tc), st, mv, start=(pas == 0 and kc == 0),
                                 stop=(pas == 2 and kc == nkc - 1))
            stg.done(mmc)
            p.wait("scalar", "tS", mmc)
            p.wait("vector", "tS", mmc)
            dh_, dl_, r = (spill_to[0], spill_to[1], oc * 128) if qkv_route is None else qkv_route(oc)
            for tc in range(NTC):
                es2, ev2 = spl.spill(bank(bpar * 4 + tc),
                                     dh_[r:r + 128, tc * 512:(tc + 1) * 512],
                                     dl_[r:r + 128, tc * 512:(tc + 1) * 512])
            epb[bpar] = (es2, ev2)

    mark("kproj")
    proj_general(w_uk, D, ckv_hi, ckv_lo, 2, (s_khi, s_klo))

    mark("vproj")
    sc, vc = stage_general(w_uv, DL, 0, D, slot=0)
    p.wait("tensor", "sS", sc)
    p.wait("tensor", "vS", vc)
    spv = SpillSlots(WH, WL, [(2048, 2048), (2560, 2560), (3072, 3072), (3584, 3584)])
    epv = {0: (0, 0), 1: (0, 0)}
    for tt in range(NKT):
        bpar = tt & 1
        p.wait("tensor", "sS", epv[bpar][0])
        p.wait("tensor", "vS", epv[bpar][1])
        mmc = 0
        for dvc in range(2):
            bk = bank(bpar * 2 + dvc)
            for pas in range(3):
                for kc in range(2):
                    if pas == 0:
                        st, mvw = ckv_hi(kc, None)[:, tt * 128:(tt + 1) * 128], WH
                    elif pas == 1:
                        st, mvw = ckv_lo(kc, None)[:, tt * 128:(tt + 1) * 128], WH
                    else:
                        st, mvw = ckv_hi(kc, None)[:, tt * 128:(tt + 1) * 128], WL
                    mv = mvw.ap()[:, kc * 1024 + dvc * 512: kc * 1024 + dvc * 512 + 512]
                    mmc = mm(bk, st, mv, start=(pas == 0 and kc == 0),
                             stop=(pas == 2 and kc == 1))
        p.wait("scalar", "tS", mmc)
        p.wait("vector", "tS", mmc)
        for dvc in range(2):
            es2, ev2 = spv.spill(bank(bpar * 2 + dvc),
                                 s_vhi[tt * 128:(tt + 1) * 128, dvc * 512:(dvc + 1) * 512],
                                 s_vlo[tt * 128:(tt + 1) * 128, dvc * 512:(dvc + 1) * 512])
        epv[bpar] = (es2, ev2)

    mark("qproj")
    proj_general(w_uq, D, xn_hi, xn_lo, NCH, (s_qhi, s_qlo))

    if "kvq" in dbg:
        p.qbarrier("sync", "gpsimd")
        p.dma("sync", dbg_t["khi"][:, :], s_khi[:, :])
        p.dma("sync", dbg_t["klo"][:, :], s_klo[:, :])
        p.dma("sync", dbg_t["qhi"][:, :], s_qhi[:, :])
        p.dma("sync", dbg_t["qlo"][:, :], s_qlo[:, :])
        p.dma("sync", dbg_t["vhi"][:, :], s_vhi[:, :])
        p.dma("sync", dbg_t["vlo"][:, :], s_vlo[:, :])
    if upto <= 2:
        p.finalize()
        return nc, di, dbg_t

    def spill_split2(src32, hi_dst, lo_dst, hi_st, lo_st, w=512):
        p.wait_now("scalar", "sS")
        sc2 = p.op("scalar", "copy", hi_st, src32)
        p.wait("vector", "sS", sc2)
        vc2 = p.op("vector", "scalar_tensor_tensor", lo_st, src32, 1.0, hi_st,
                   op0=AluOpType.mult, op1=AluOpType.subtract)
        p.wait("gpsimd", "sS", sc2)
        p.wait("gpsimd", "vS", vc2)
        p.dma("gpsimd", hi_dst, hi_st)
        p.dma("gpsimd", lo_dst, lo_st)
        return sc2, vc2

    mark("s1")
    conv_work = []
    conv_rel = [0, 0]

    def convert_fp16(src, dst, rows, cols):
        cw = min(2048, cols)
        for r in range(rows // 128):
            for half in range(cols // cw):
                def item(src=src, dst=dst, r=r, half=half, cw=cw):
                    p.wait("sync", "sS", conv_rel[0])
                    p.wait("sync", "vS", conv_rel[1])
                    p.dma("sync", WS32.ap()[:, 0:cw], src[r * 128:(r + 1) * 128, half * cw:(half + 1) * cw])
                    p.qwait("scalar", "sync")
                    p.qwait("scalar", "gpsimd")
                    sc = p.op("scalar", "copy", WL.ap()[:, 2048:2048 + cw], WS32.ap()[:, 0:cw])
                    conv_rel[0] = sc
                    p.wait("gpsimd", "sS", sc)
                    p.dma("gpsimd", dst[r * 128:(r + 1) * 128, half * cw:(half + 1) * cw],
                          WL.ap()[:, 2048:2048 + cw])
                conv_work.append(item)

    conv_done = [0]

    def pump(n, drain=True):
        n = min(n, len(conv_work))
        if n == 0:
            return
        if drain:
            p.wait_now("sync", "sS")
            p.wait_now("sync", "vS")
            conv_rel[0] = max(conv_rel[0], p.cnt["sS"])
            conv_rel[1] = max(conv_rel[1], p.cnt["vS"])
        for _ in range(n):
            conv_work.pop(0)()
            conv_done[0] += 1

    def pump_all():
        pump(len(conv_work))

    def convert_split_fp16(src, dsthi, dstlo, rows, cols):
        cw = min(2048, cols)
        for r in range(rows // 128):
            for half in range(cols // cw):
                def item(src=src, dsthi=dsthi, dstlo=dstlo, r=r, half=half, cw=cw):
                    assert cw <= 1024
                    p.wait("sync", "sS", conv_rel[0])
                    p.wait("sync", "vS", conv_rel[1])
                    p.dma("sync", WS32.ap()[:, 0:cw], src[r * 128:(r + 1) * 128, half * cw:(half + 1) * cw])
                    p.qwait("scalar", "sync")
                    p.qwait("scalar", "gpsimd")
                    sc = p.op("scalar", "copy", WL.ap()[:, 2048:2048 + cw], WS32.ap()[:, 0:cw])
                    p.wait("vector", "sS", sc)
                    p.qwait("vector", "gpsimd")
                    vc = p.op("vector", "scalar_tensor_tensor", WL.ap()[:, 2048 + cw:2048 + 2 * cw],
                              WS32.ap()[:, 0:cw], 1.0, WL.ap()[:, 2048:2048 + cw],
                              op0=AluOpType.mult, op1=AluOpType.subtract)
                    conv_rel[0] = sc
                    conv_rel[1] = vc
                    p.wait("gpsimd", "sS", sc)
                    p.wait("gpsimd", "vS", vc)
                    p.dma("gpsimd", dsthi[r * 128:(r + 1) * 128, half * cw:(half + 1) * cw],
                          WL.ap()[:, 2048:2048 + cw])
                    p.dma("gpsimd", dstlo[r * 128:(r + 1) * 128, half * cw:(half + 1) * cw],
                          WL.ap()[:, 2048 + cw:2048 + 2 * cw])
                conv_work.append(item)

    ep = {0: (0, 0), 1: (0, 0)}
    s1_tdone = [p.cnt["tS"], p.cnt["tS"]]
    s1_ws = [p.cnt["sS"], p.cnt["sS"]]
    sp1 = SpillSlots(WL, WL, [(0, 1024), (512, 1536)])
    for dh in range(32):
        par = dh & 1
        base = par * 2048
        scl = 0
        for j, wt_ in enumerate((w1t, w2t)):
            srcap = _setap(wt_, [[DH, 128], [128 * DH, NCH], [1, 128]], offset=dh * 128)
            p.wait("sync", "sS", s1_ws[j])
            p.dma("sync", WS32.ap()[:, j * 1024:(j + 1) * 1024], srcap)
            p.qwait("scalar", "sync")
            p.wait("scalar", "tS", s1_tdone[par])
            scl = p.op("scalar", "copy", WH.ap()[:, base + j * 1024:base + (j + 1) * 1024],
                       WS32.ap()[:, j * 1024:(j + 1) * 1024])
            s1_ws[j] = scl
        p.wait("tensor", "sS", scl)
        b1, b2 = bank(par * 2), bank(par * 2 + 1)
        for tc in range(NTC):
            mmc = 0
            p.wait("tensor", "sS", ep[par][0])
            p.wait("tensor", "vS", ep[par][1])
            for pas in range(2):
                for kc in range(NCH):
                    mv = xn_hi(kc, tc) if pas == 0 else xn_lo(kc, tc)
                    mm(b1, WH.ap()[:, base + kc * 128:base + kc * 128 + 128], mv,
                       start=(pas == 0 and kc == 0), stop=(pas == 1 and kc == NCH - 1))
                    mmc = mm(b2, WH.ap()[:, base + 1024 + kc * 128:base + 1024 + kc * 128 + 128], mv,
                             start=(pas == 0 and kc == 0), stop=(pas == 1 and kc == NCH - 1))
            s1_tdone[par] = mmc
            p.wait("scalar", "tS", mmc)
            u = SC32.ap()[:, par * 1024:par * 1024 + 512]
            v = SC32.ap()[:, par * 1024 + 512:par * 1024 + 1024]
            a32 = MISC.ap()[:, par * 1024:par * 1024 + 512]
            p.wait_now("scalar", "sS")
            su = p.op("scalar", "activation", u, b1, AF.Silu,
                      scale=SVEC.ap()[:, dh:dh + 1])
            sv = p.op("scalar", "activation", v, b2, AF.Copy,
                      scale=SVEC.ap()[:, 32 + dh:32 + dh + 1])
            p.wait("vector", "sS", sv)
            p.wait_now("vector", "vS")
            va = p.op("vector", "tensor_tensor", a32, u, v, op=AluOpType.mult)
            p.wait("scalar", "vS", va)
            es, ev = sp1.spill(a32,
                               s_ahi[dh * 128:(dh + 1) * 128, tc * 512:(tc + 1) * 512],
                               s_alo[dh * 128:(dh + 1) * 128, tc * 512:(tc + 1) * 512])
            ep[par] = (es, ev)
    if "affn" in dbg:
        p.qbarrier("sync", "gpsimd")
        p.dma("sync", dbg_t["ahi"][:, :], s_ahi[:, :])
        p.dma("sync", dbg_t["alo"][:, :], s_alo[:, :])
    if upto <= 3:
        p.finalize()
        return nc, di, dbg_t

    def attn_core(layer, pump_per_hp=0):
        ep_bank = {i: (0, 0) for i in range(4)}
        ep_ctx = {}
        avdone = {}
        ep_p32 = {}
        spc = SpillSlots(CKV, CKV, [(4224, 4736), (5248, 5760)], rows=64)

        def phi_ap(h2, par):
            return pa(8192 + (h2 * 4 + par) * 512, 512)

        def plo_ap(h2, par):
            return pa(20736 + (h2 * 4 + par) * 512, 512)

        def gsh_ap(h2, off, w=512):
            return pa(12288 + h2 * 4224 + off, w)

        for hp in range(NCH):
            p.qbarrier("sync", "gpsimd")
            p.wait_now("sync", "tS")
            p.dma("sync", pa(0, 2048), s_khi[hp * 128:(hp + 1) * 128, :])
            p.dma("sync", pa(2048, 2048), s_klo[hp * 128:(hp + 1) * 128, :])
            p.dma("sync", pa(4096, 2048), s_qhi[hp * 128:(hp + 1) * 128, :])
            p.dma("sync", pa(6144, 2048), s_qlo[hp * 128:(hp + 1) * 128, :])
            for h2 in range(2):
                h = 2 * hp + h2
                vsrc_hi = _setap(s_vhi, [[1024, 128], [128 * 1024, NKT], [1, 64]],
                                 offset=64 * h)
                vsrc_lo = _setap(s_vlo, [[1024, 128], [128 * 1024, NKT], [1, 64]],
                                 offset=64 * h)
                dst_hi = _setap(CKV.ap(), [[8192, 128], [66, NKT], [1, 64]],
                                offset=h2 * 2112)
                dst_lo = _setap(CKV.ap(), [[8192, 128], [66, NKT], [1, 64]],
                                offset=h2 * 2112 + 1056)
                p.dma("sync", dst_hi, vsrc_hi)
                p.dma("sync", dst_lo, vsrc_lo)
            p.qwait("vector", "sync")
            on1 = 0
            for h2 in range(2):
                p.op("vector", "memset",
                     _setap(CKV.ap(), [[8192, 128], [66, NKT], [1, 1]], offset=h2 * 2112 + 64), 1.0)
                on1 = p.op("vector", "memset",
                           _setap(CKV.ap(), [[8192, 128], [66, NKT], [1, 1]], offset=h2 * 2112 + 1056 + 64), 0.0)
            p.wait("tensor", "vS", on1)
            p.qwait("tensor", "sync")
            if layer == 1:
                for h2 in range(2):
                    h = 2 * hp + h2
                    p.wait_now("gpsimd", "tS")
                    gsrc = _setap(GSH1.ap(), [[4224, 1], [0, 128], [1, 4224]],
                                  offset=h * 4224)
                    gdst = _setap(s_gsh[h2], [[4224, 128], [1, 4224]])
                    p.dma("gpsimd", gdst, gsrc)
                    p.qwait("gpsimd", "gpsimd")
                    rsrc = _setap(s_gsh[h2], [[4223, 128], [1, 4223]])
                    p.dma("gpsimd", gsh_ap(h2, 0, 4223), rsrc)
                p.qwait("tensor", "gpsimd")
                mxv = max([0] + [v for v in ep_ctx.values()])
                mx_done = {}
                for qt in range(NKT):
                    for h2 in range(2):
                        base = 64 * h2
                        for ks in range(NTC):
                            bmx = bank(4 + h2 * 2 + (ks & 1))
                            p.wait("tensor", "vS", mx_done.get((h2, ks & 1), mxv))
                            qst = pa(4096 + qt * 128, 128, base, base + 64)
                            kmv = pa(0 + ks * 512, 512, base, base + 64)
                            mmx = mm(bmx, qst, kmv, start=True, stop=True)
                            p.wait("vector", "tS", mmx)
                            p.wait_now("vector", "vS")
                            col = 28 + h2 * 16 + qt
                            if ks == 0:
                                mxc = p.op("vector", "tensor_reduce",
                                           MISC.ap()[:, col:col + 1], bmx,
                                           axis=AX.X, op=AluOpType.max)
                            else:
                                p.op("vector", "tensor_reduce",
                                     MISC.ap()[:, 60 + h2:61 + h2], bmx, axis=AX.X,
                                     op=AluOpType.max)
                                p.wait_now("vector", "vS")
                                mxc = p.op("vector", "tensor_tensor",
                                           MISC.ap()[:, col:col + 1],
                                           MISC.ap()[:, col:col + 1],
                                           MISC.ap()[:, 60 + h2:61 + h2], op=AluOpType.max)
                            mx_done[(h2, ks & 1)] = mxc
                p.wait_now("tensor", "vS")
                trc = 0
                for h2 in range(2):
                    trc = p.op("tensor", "transpose", bank(4 + h2 * 2, 0, 16, 128),
                               MISC.ap()[:, 28 + h2 * 16:44 + h2 * 16], IDF.ap())
                p.wait("vector", "tS", trc)
                for h2 in range(2):
                    mrow = 32 if h2 == 0 else 64
                    p.wait_now("vector", "vS")
                    p.op("vector", "tensor_copy", MISC.ap()[mrow:mrow + 16, 0:128],
                         bank(4 + h2 * 2, 0, 16, 128))
                crw = 0
                for h2 in range(2):
                    mrow = 32 if h2 == 0 else 64
                    p.wait_now("gpsimd", "vS")
                    fsrc = _setap(MISC.ap(), [[2048, 16], [1, 128]], offset=mrow * 2048)
                    fdst = _setap(MISC.ap(), [[2048, 1], [1, 2048]], offset=96 * 2048)
                    p.dma("gpsimd", fdst, fsrc)
                    p.qwait("vector", "gpsimd")
                    crow_dst = (WL if h2 == 0 else WH).ap()[0:1, 0:2048]
                    crw = p.op("vector", "tensor_scalar", crow_dst,
                               MISC.ap()[96:97, 0:2048],
                               -1.0, EXPSHIFT, op0=AluOpType.mult, op1=AluOpType.add)
                p.wait("tensor", "vS", crw)
            for qc in range(NTC):
                kts = range(4 * qc + 4) if layer == 0 else range(NKT)
                nk_ = len(kts)
                first_av = {0: True, 1: True}
                ep_slot = {}

                def emit_av(i, kt):
                    par = i % 4
                    for h2 in range(2):
                        cslot = 4 + 2 * h2 + (qc & 1)
                        bC = bank(cslot, 0, 65)
                        es = ep_slot[(h2, par)]
                        p.wait("tensor", "sS", es[0])
                        p.wait("tensor", "vS", es[1])
                        if first_av[h2]:
                            p.wait("tensor", "vS", ep_ctx.get((h2, qc & 1), 0))
                        vhi_st = ckv_ap(h2 * 2112 + kt * 66, 65)
                        vlo_st = ckv_ap(h2 * 2112 + 1056 + kt * 66, 65)
                        mm(bC, vhi_st, phi_ap(h2, par), start=first_av[h2], stop=False)
                        mm(bC, vhi_st, plo_ap(h2, par), start=False, stop=False)
                        avdone[(h2, par)] = mm(bC, vlo_st, phi_ap(h2, par), start=False,
                                               stop=(i == nk_ - 1))
                        first_av[h2] = False

                pend = []
                for i, kt in enumerate(kts):
                    par = i % 4
                    for h2 in range(2):
                        base = 64 * h2
                        bS = h2 * 2 + (i & 1)
                        p.wait("tensor", "sS", ep_bank[bS][0])
                        p.wait("tensor", "vS", ep_bank[bS][1])
                        straddle = (layer == 0 and kt >= 4 * qc)
                        st0 = (not straddle) and (layer == 0)
                        if straddle:
                            mm(bank(bS), IDB.ap(),
                               CMSK.ap()[:, (kt - 4 * qc) * 512:(kt - 4 * qc + 1) * 512],
                               start=True, stop=False)
                        if layer == 1:
                            off = qc * 512 - kt * 128 + 2047
                            mm(bank(bS), IDB.ap(), gsh_ap(h2, off), start=True, stop=False)
                            crow = (WL if h2 == 0 else WH).ap()[0:1, qc * 512:(qc + 1) * 512]
                            mm(bank(bS), ONE1.ap()[0:1, 0:128], crow, start=False, stop=False)
                        kh = pa(0 + kt * 128, 128, base, base + 64)
                        kl = pa(2048 + kt * 128, 128, base, base + 64)
                        qh = pa(4096 + qc * 512, 512, base, base + 64)
                        ql = pa(6144 + qc * 512, 512, base, base + 64)
                        mm(bank(bS), kh, qh, start=st0, stop=False)
                        mm(bank(bS), kh, ql, start=False, stop=False)
                        mm(bank(bS), kl, qh, start=False, stop=True)
                    qk_done = p.cnt["tS"]
                    if len(pend) == 2:
                        emit_av(*pend.pop(0))
                    pend.append((i, kt))
                    for h2 in range(2):
                        bS = h2 * 2 + (i & 1)
                        p.wait("scalar", "tS", qk_done)
                        p.wait("scalar", "tS", avdone.get((h2, par), 0))
                        p32 = SC32.ap()[:, (h2 * 2 + (i & 1)) * 512:(h2 * 2 + (i & 1)) * 512 + 512]
                        p.wait_now("scalar", "sS")
                        p.wait("scalar", "vS", ep_p32.get((h2, i & 1), 0))
                        se = p.op("scalar", "activation", p32, bank(bS), AF.Exp,
                                  scale=0.125)
                        phi = phi_ap(h2, par)
                        plo = plo_ap(h2, par)
                        p.wait("scalar", "sS", se)
                        sh = p.op("scalar", "copy", phi, p32)
                        p.wait("vector", "sS", sh)
                        p.wait("vector", "tS", avdone.get((h2, par), 0))
                        vl = p.op("vector", "scalar_tensor_tensor", plo, p32, 1.0,
                                  phi, op0=AluOpType.mult, op1=AluOpType.subtract)
                        ep_bank[bS] = (sh, vl)
                        ep_slot[(h2, par)] = (sh, vl)
                        ep_p32[(h2, i & 1)] = vl
                for pe_ in pend:
                    emit_av(*pe_)
                for h2 in range(2):
                    h = 2 * hp + h2
                    cslot = 4 + 2 * h2 + (qc & 1)
                    p.wait("vector", "tS", avdone[(h2, (nk_ - 1) % 4)])
                    p.wait_now("vector", "vS")
                    p.wait_now("vector", "sS")
                    rz = MISC.ap()[0:1, 1536:2048]
                    p.op("vector", "reciprocal", rz, bank(cslot, 64, 65))
                    p.wait_now("sync", "vS")
                    zb = MISC.ap()[0:64, 512:1024]
                    bcast_row(zb, rz, 512, q="sync")
                    p.qwait("vector", "sync")
                    ctx32 = MISC.ap()[0:64, 1024:1536]
                    vc3 = p.op("vector", "tensor_tensor", ctx32, bank(cslot, 0, 64),
                               zb, op=AluOpType.mult)
                    ep_ctx[(h2, qc & 1)] = vc3
                    p.wait("scalar", "vS", vc3)
                    spc.spill(ctx32,
                              s_ctxhi[h * 64:(h + 1) * 64, qc * 512:(qc + 1) * 512],
                              s_ctxlo[h * 64:(h + 1) * 64, qc * 512:(qc + 1) * 512])
                if pump_per_hp:
                    pump(max(1, pump_per_hp // 4))


    convert_fp16(h1t, s_h1, DH, DH)
    convert_fp16(h2t, s_h2, DH, DH)
    convert_fp16(w3t, s_w3, DH, D)
    mark("attn0")
    attn_core(0, pump_per_hp=8)
    if "ctx0" in dbg:
        p.qbarrier("sync", "gpsimd")
        p.dma("sync", dbg_t["ctxhi"][:, :], s_ctxhi[:, :])
        p.dma("sync", dbg_t["ctxlo"][:, :], s_ctxlo[:, :])

    def out_proj(wsrc):
        p.wait_now("sync", "tS")
        p.qbarrier("sync", "gpsimd")
        for c in range(NCH):
            p.dma("sync", pa(c * 2048, 2048), s_ctxhi[c * 128:(c + 1) * 128, :])
            p.dma("sync", pa(16384 + c * 2048, 2048), s_ctxlo[c * 128:(c + 1) * 128, :])
        p.qwait("tensor", "sync")
        stg = Stager()
        epo = {0: 0, 1: 0}
        for oc in range(NCH):
            slot = stg.stage(wsrc, D, oc * 128, 128)
            bpar = oc & 1
            p.wait("tensor", "vS", epo[bpar])
            mmc = 0
            for pas in range(3):
                for kc in range(NCH):
                    st = (WH if pas < 2 else WL).ap()[:, slot + kc * 128:slot + kc * 128 + 128]
                    for tc in range(NTC):
                        mv = pa(kc * 2048 + tc * 512, 512) if pas != 1 else pa(16384 + kc * 2048 + tc * 512, 512)
                        mmc = mm(bank(bpar * 4 + tc), st, mv, start=(pas == 0 and kc == 0),
                                 stop=(pas == 2 and kc == NCH - 1))
            stg.done(mmc)
            p.wait("vector", "tS", mmc)
            for tc in range(NTC):
                epo[bpar] = p.op("vector", "scalar_tensor_tensor", X[oc].ap()[:, tc * 512:(tc + 1) * 512],
                                 bank(bpar * 4 + tc), 1.0, X[oc].ap()[:, tc * 512:(tc + 1) * 512],
                                 op0=AluOpType.mult, op1=AluOpType.add)

    mark("wo")
    out_proj(w_o)
    if "x1a" in dbg:
        p.wait_now("gpsimd", "vS")
        for c in range(NCH):
            p.dma("gpsimd", dbg_t["x1a"][c * 128:(c + 1) * 128, :], X[c].ap())
    if upto <= 4:
        p.finalize()
        return nc, di, dbg_t

    mark("convert")
    pump(max(0, 64 - conv_done[0]))

    def hstage(src16, movh_base, movl_base, dsthi, dstlo, scol, load_src_hi, load_src_lo):
        ep2 = {i: (0, 0) for i in range(4)}
        sph = SpillSlots(WL, WL, [(0, 1024), (512, 1536)])
        hst_tdone = [p.cnt["tS"], p.cnt["tS"]]
        t32v = [p.cnt["vS"], p.cnt["vS"]]
        for tc in range(NTC):
            p.wait_now("sync", "tS")
            p.wait_now("sync", "sS")
            p.wait_now("sync", "vS")
            p.qbarrier("sync", "gpsimd")
            src_hi = _setap(load_src_hi, [[T, 128], [128 * T, 32], [1, 512]],
                            offset=tc * 512)
            src_lo = _setap(load_src_lo, [[T, 128], [128 * T, 32], [1, 512]],
                            offset=tc * 512)
            dst_hi = _setap(PA.ap(), [[32768, 128], [512, 32], [1, 512]], offset=0)
            dst_lo = _setap(PA.ap(), [[32768, 128], [512, 32], [1, 512]], offset=16384)
            p.dma("sync", dst_hi, src_hi)
            p.dma("sync", dst_lo, src_lo)
            p.qwait("tensor", "sync")
            for oc in range(32):
                if oc & 1:
                    pump(1, drain=False)
                bk = bank(oc & 3)
                p.wait("tensor", "sS", ep2[oc & 3][0])
                p.wait("tensor", "vS", ep2[oc & 3][1])
                for khalf in range(2):
                    par = khalf
                    p.wait("sync", "tS", hst_tdone[par])
                    src = _setap(src16, [[src16.shape[1], 128], [128 * src16.shape[1], 16], [1, 128]],
                                 offset=khalf * 2048 * src16.shape[1] + oc * 128)
                    p.dma("sync", WH.ap()[:, par * 2048:par * 2048 + 2048], src)
                    p.qwait("tensor", "sync")
                    hst_tdone[par] = p.cnt["tS"] + 32
                    for pas in range(2):
                        for k2 in range(16):
                            kc = khalf * 16 + k2
                            mv = pa((0 if pas == 0 else 16384) + kc * 512, 512)
                            mm(bk, WH.ap()[:, par * 2048 + k2 * 128:par * 2048 + k2 * 128 + 128],
                               mv, start=(khalf == 0 and pas == 0 and k2 == 0),
                               stop=(khalf == 1 and pas == 1 and k2 == 15))
                mmc = p.cnt["tS"]
                p.wait("scalar", "tS", mmc)
                p.wait_now("scalar", "sS")
                p.wait("scalar", "vS", t32v[oc & 1])
                t32 = SC32.ap()[:, (oc & 1) * 512:(oc & 1) * 512 + 512]
                p.op("scalar", "activation", t32, bk, AF.Gelu,
                     scale=SVEC.ap()[:, scol + oc:scol + oc + 1])
                ep2[oc & 3] = sph.spill(t32,
                                        dsthi[oc * 128:(oc + 1) * 128, tc * 512:(tc + 1) * 512],
                                        dstlo[oc * 128:(oc + 1) * 128, tc * 512:(tc + 1) * 512])
                t32v[oc & 1] = ep2[oc & 3][1]

    mark("s2")
    hstage(s_h1, 0, 16384, s_bhi, s_blo, 64, s_ahi, s_alo)
    mark("s3")
    hstage(s_h2, 0, 16384, s_ahi, s_alo, 96, s_bhi, s_blo)

    pump_all()
    p.wait_now("vector", "sS")
    p.wait_now("gpsimd", "sS")
    mark("s4")
    ep_v4 = 0
    s4_tdone = [p.cnt["tS"], p.cnt["tS"]]
    for tc in range(NTC):
        p.wait_now("sync", "tS")
        p.wait_now("sync", "vS")
        p.qbarrier("sync", "gpsimd")
        src_hi = _setap(s_ahi, [[T, 128], [128 * T, 32], [1, 512]], offset=tc * 512)
        src_lo = _setap(s_alo, [[T, 128], [128 * T, 32], [1, 512]], offset=tc * 512)
        dst_hi = _setap(PA.ap(), [[32768, 128], [512, 32], [1, 512]], offset=0)
        dst_lo = _setap(PA.ap(), [[32768, 128], [512, 32], [1, 512]], offset=16384)
        p.dma("sync", dst_hi, src_hi)
        p.dma("sync", dst_lo, src_lo)
        p.qwait("tensor", "sync")
        for oc in range(NCH):
            bk = bank(oc & 3)
            p.wait("tensor", "vS", ep_v4)
            mmc = 0
            for khalf in range(2):
                p.wait("sync", "tS", s4_tdone[khalf])
                src = _setap(s_w3, [[D, 128], [128 * D, 16], [1, 128]],
                             offset=khalf * 2048 * D + oc * 128)
                p.dma("sync", WH.ap()[:, khalf * 2048:khalf * 2048 + 2048], src)
                p.qwait("tensor", "sync")
                for pas in range(2):
                    for k2 in range(16):
                        kc = khalf * 16 + k2
                        mv = pa((0 if pas == 0 else 16384) + kc * 512, 512)
                        mmc = mm(bk, WH.ap()[:, khalf * 2048 + k2 * 128:khalf * 2048 + k2 * 128 + 128],
                                 mv, start=(khalf == 0 and pas == 0 and k2 == 0),
                                 stop=(khalf == 1 and pas == 1 and k2 == 15))
                s4_tdone[khalf] = mmc
            p.wait("vector", "tS", mmc)
            ep_v4 = p.op("vector", "scalar_tensor_tensor",
                         X[oc].ap()[:, tc * 512:(tc + 1) * 512], bk,
                         SVEC.ap()[:, 128 + oc:128 + oc + 1],
                         X[oc].ap()[:, tc * 512:(tc + 1) * 512],
                         op0=AluOpType.mult, op1=AluOpType.add)

    if "x1" in dbg:
        p.wait_now("gpsimd", "vS")
        for c in range(NCH):
            p.dma("gpsimd", dbg_t["x1"][c * 128:(c + 1) * 128, :], X[c].ap())
    if upto <= 5:
        p.finalize()
        return nc, di, dbg_t

    mark("pos")
    p.wait_now("sync", "vS")
    p.wait_now("sync", "sS")
    p.dma("sync", SC32.ap()[0:101, 0:2048], protod_in[:, :])
    p.dma("sync", MISC.ap()[0:101, 0:4], coeffsT[:, :])
    p.dma("sync", MISC.ap()[0:4, 16:32], wposT[:, :])
    p.dma("sync", CMSK.ap()[0:1, 0:2048], tie_in[:, :])
    p.qwait("tensor", "sync")
    mmp = 0
    for tc in range(NTC):
        mmp = mm(bank(tc, 0, 4), MISC.ap()[0:101, 0:4], SC32.ap()[0:101, tc * 512:(tc + 1) * 512],
                 start=True, stop=False)
    p.wait("sync", "tS", mmp)
    p.dma("sync", SC32.ap()[0:101, 0:2048], protod_lo_in[:, :])
    p.qwait("tensor", "sync")
    for tc in range(NTC):
        mm(bank(tc, 0, 4), MISC.ap()[0:101, 0:4], SC32.ap()[0:101, tc * 512:(tc + 1) * 512],
           start=False, stop=True)
    p.wait_now("vector", "tS")
    vtd = 0
    for tc in range(NTC):
        vtd = p.op("vector", "tensor_copy", SC32.ap()[0:4, tc * 512:(tc + 1) * 512],
                   bank(tc, 0, 4))
    p.wait("tensor", "vS", vtd)
    for tc in range(NTC):
        mm(bank(4 + tc, 0, 16), MISC.ap()[0:4, 16:32],
           SC32.ap()[0:4, tc * 512:(tc + 1) * 512], start=True, stop=True)
    p.wait_now("scalar", "tS")
    sg8 = 0
    for tc in range(NTC):
        sg8 = p.op("scalar", "activation", MISC.ap()[64:80, tc * 512:(tc + 1) * 512],
                   bank(4 + tc, 0, 16), AF.Copy, scale=8.0)
    p.wait("vector", "sS", sg8)
    gl = p.op("vector", "tensor_copy",
              GSH1.ap()[0:16, 0:2047],
              _setap(MISC.ap(), [[2048, 16], [-1, 2047]], offset=64 * 2048 + 2047))
    p.wait_now("vector", "vS")
    gr = p.op("vector", "tensor_copy", GSH1.ap()[0:16, 2047:4095],
              MISC.ap()[64:80, 0:2048])
    gz = p.op("vector", "memset", GSH1.ap()[0:16, 4095:4224], 0.0)
    if "gsh" in dbg:
        p.wait_now("gpsimd", "vS")
        p.dma("gpsimd", dbg_t["gsh"][:, :], GSH1.ap()[0:16, 0:4224])

    p.wait_now("scalar", "tS")
    p.wait_now("vector", "tS")
    mark("ln1")
    ln_phase("ln1")
    if "xn1" in dbg:
        p.wait_now("gpsimd", "vS")
        p.wait_now("gpsimd", "sS")
        for c in range(NCH):
            p.dma("gpsimd", dbg_t["xn1"][c * 128:(c + 1) * 128, 0:T], xn_hi(c))
            p.dma("gpsimd", dbg_t["xn1"][c * 128:(c + 1) * 128, T:2 * T], xn_lo(c))

    mark("qkv")
    def qkv_route(oc):
        if oc < 8:
            return s_qhi, s_qlo, oc * 128
        return s_khi, s_klo, (oc - 8) * 128

    proj_general(wqkvt, 16 * 128, xn_hi, xn_lo, NCH, None, qkv_route=qkv_route)

    mark("l1v")
    for dvc in range(2):
        sc1, vc1 = stage_general(wqkvt, 512, 2048 + dvc * 512, 512, slot=0, row0=0)
        sc2, vc2 = stage_general(wqkvt, 512, 2048 + dvc * 512, 512, slot=2048, row0=512)
        p.wait("tensor", "sS", sc2)
        p.wait("tensor", "vS", vc2)
        if dvc == 0 and "vslab" in dbg:
            p.wait_now("gpsimd", "sS")
            p.wait_now("gpsimd", "vS")
            p.dma("gpsimd", dbg_t["vslabh"][:, :], WH.ap())
            p.dma("gpsimd", dbg_t["vslabl"][:, :], WL.ap())
            p.qwait("tensor", "gpsimd")
        spv1 = SpillSlots(CKV, CKV, [(4096, 4608), (5120, 5632), (6144, 6656), (7168, 7680)])
        epv1 = {k: (0, 0) for k in range(4)}
        for tt in range(NKT):
            bpar = tt & 3
            bk = bank(bpar)
            p.wait("tensor", "sS", epv1[bpar][0])
            p.wait("tensor", "vS", epv1[bpar][1])
            mmc = 0
            for pas in range(3):
                for kc in range(NCH):
                    slot = 0 if kc < 4 else 2048
                    k2 = kc % 4
                    wsl = (WH if pas < 2 else WL).ap()[:, slot + k2 * 512: slot + k2 * 512 + 512]
                    stt = (xn_hi(kc) if pas != 1 else xn_lo(kc))[:, tt * 128:(tt + 1) * 128]
                    mmc = mm(bk, stt, wsl, start=(pas == 0 and kc == 0),
                             stop=(pas == 2 and kc == NCH - 1))
            p.wait("scalar", "tS", mmc)
            p.wait("vector", "tS", mmc)
            epv1[bpar] = spv1.spill(bk,
                                    s_vhi[tt * 128:(tt + 1) * 128, dvc * 512:(dvc + 1) * 512],
                                    s_vlo[tt * 128:(tt + 1) * 128, dvc * 512:(dvc + 1) * 512])

    if "kvq1" in dbg:
        p.qbarrier("sync", "gpsimd")
        p.dma("sync", dbg_t["khi"][:, :], s_khi[:, :])
        p.dma("sync", dbg_t["klo"][:, :], s_klo[:, :])
        p.dma("sync", dbg_t["qhi"][:, :], s_qhi[:, :])
        p.dma("sync", dbg_t["qlo"][:, :], s_qlo[:, :])
        p.dma("sync", dbg_t["vhi"][:, :], s_vhi[:, :])
        p.dma("sync", dbg_t["vlo"][:, :], s_vlo[:, :])
    if upto <= 6:
        p.finalize()
        return nc, di, dbg_t

    mark("logits")
    sc, vc = stage_general(ctrl, D, 0, 64, slot=0)
    p.wait("tensor", "sS", sc)
    p.wait("tensor", "vS", vc)
    p.wait_now("tensor", "vS")
    mmc = 0
    for tc in range(NTC):
        mm(bank(tc, 0, 64), ONE1.ap()[0:1, 0:64],
           CMSK.ap()[0:1, tc * 512:(tc + 1) * 512], start=True, stop=False)
        for pas in range(3):
            for kc in range(NCH):
                st = (WH if pas < 2 else WL).ap()[:, kc * 64:(kc + 1) * 64]
                mv = xn_hi(kc, tc) if pas != 1 else xn_lo(kc, tc)
                mmc = mm(bank(tc, 0, 64), st, mv, start=False,
                         stop=(pas == 2 and kc == NCH - 1))
    p.wait("vector", "tS", mmc)
    vlt = 0
    for tc in range(NTC):
        vlt = p.op("vector", "tensor_copy", SC32.ap()[0:64, tc * 512:(tc + 1) * 512],
                   bank(tc, 0, 64))
    p.wait("vector", "vS", vlt)
    gmx = p.op("vector", "tensor_reduce", MISC.ap()[0:64, 0:128],
               SC32.ap()[0:64, :].rearrange("p (g t) -> p g t", t=16),
               axis=AX.X, op=AluOpType.max)
    p.wait("vector", "vS", gmx)
    prm = p.op("vector", "tensor_tensor",
               CKV.ap()[0:64, 0:2048].rearrange("p (g t) -> p g t", t=16),
               SC32.ap()[0:64, :].rearrange("p (g t) -> p g t", t=16),
               MISC.ap()[0:64, 0:128].broadcast_to([64, 128, 16]),
               op=AluOpType.is_equal)
    if "perm" in dbg:
        p.wait("gpsimd", "vS", prm)
        p.dma("gpsimd", dbg_t["perm"][:, :], CKV.ap()[0:64, 0:2048])
        p.dma("gpsimd", dbg_t["lg"][:, :], SC32.ap()[0:64, 0:2048])

    mark("y")
    ep_y = {k: (0, 0) for k in range(8)}
    stgy = Stager()
    spy = SpillSlots(WH, WL, [(2048, 2048), (2560, 2560), (3072, 3072), (3584, 3584)])
    vm_last = prm
    for oc in range(32):
        slot = stgy.stage(f1m, D, oc * 128, 128)
        p.wait("gpsimd", "vS", vm_last)
        psrc = _setap(CKV.ap(), [[8192, 2], [0, 64], [1, 2048]], offset=2 * oc * 8192)
        p.dma("gpsimd", ckv_ap(2048, 2048), psrc)
        p.qwait("tensor", "gpsimd")
        for tc in range(NTC):
            bidx = (oc & 1) * 4 + tc
            bk = bank(bidx)
            p.wait("tensor", "sS", ep_y[bidx][0])
            p.wait("tensor", "vS", ep_y[bidx][1])
            mmc = 0
            for pas in range(3):
                for kc in range(NCH):
                    st = (WH if pas < 2 else WL).ap()[:, slot + kc * 128:slot + kc * 128 + 128]
                    mv = xn_hi(kc, tc) if pas != 1 else xn_lo(kc, tc)
                    mmc = mm(bk, st, mv, start=(pas == 0 and kc == 0),
                             stop=(pas == 2 and kc == NCH - 1))
            if tc == NTC - 1:
                stgy.done(mmc)
            p.wait("scalar", "tS", mmc)
            p.wait_now("scalar", "sS")
            t32 = SC32.ap()[:, (tc & 1) * 512:(tc & 1) * 512 + 512]
            sr = p.op("scalar", "activation", t32, bk, AF.Relu,
                      bias=SVEC.ap()[:, 136 + oc:136 + oc + 1], scale=1.0)
            p.wait("vector", "sS", sr)
            p.wait_now("vector", "vS")
            m32 = MISC.ap()[:, (tc & 1) * 512:(tc & 1) * 512 + 512]
            vm = p.op("vector", "tensor_tensor", m32, t32,
                      ckv_ap(2048 + tc * 512, 512), op=AluOpType.mult)
            p.wait("scalar", "vS", vm)
            vm_last = vm
            ep_y[bidx] = spy.spill(m32,
                                   s_ahi[oc * 128:(oc + 1) * 128, tc * 512:(tc + 1) * 512],
                                   s_alo[oc * 128:(oc + 1) * 128, tc * 512:(tc + 1) * 512])
    if "ym" in dbg:
        p.qbarrier("sync", "gpsimd")
        p.dma("sync", dbg_t["ymhi"][:, :], s_ahi[:, :])
        p.dma("sync", dbg_t["ymlo"][:, :], s_alo[:, :])
    if upto <= 7:
        p.finalize()
        return nc, di, dbg_t

    convert_split_fp16(f2m, s_f2hi, s_f2lo, DH, D)
    mark("attn1")
    attn_core(1, pump_per_hp=8)
    mark("wout")
    out_proj(woutt)
    if "x2a" in dbg:
        p.wait_now("gpsimd", "vS")
        for c in range(NCH):
            p.dma("gpsimd", dbg_t["x2a"][c * 128:(c + 1) * 128, :], X[c].ap())
    if upto <= 8:
        p.finalize()
        return nc, di, dbg_t


    mark("f2conv")
    pump_all()
    mark("f2")
    ep_v5 = 0
    f2_tdone = [p.cnt["tS"], p.cnt["tS"]]
    for tc in range(NTC):
        p.wait_now("sync", "tS")
        p.wait_now("sync", "vS")
        p.qbarrier("sync", "gpsimd")
        src_hi = _setap(s_ahi, [[T, 128], [128 * T, 32], [1, 512]], offset=tc * 512)
        src_lo = _setap(s_alo, [[T, 128], [128 * T, 32], [1, 512]], offset=tc * 512)
        dst_hi = _setap(PA.ap(), [[32768, 128], [512, 32], [1, 512]], offset=0)
        dst_lo = _setap(PA.ap(), [[32768, 128], [512, 32], [1, 512]], offset=16384)
        p.dma("sync", dst_hi, src_hi)
        p.dma("sync", dst_lo, src_lo)
        p.qwait("tensor", "sync")
        for oc in range(NCH):
            bk = bank(oc & 3)
            p.wait("tensor", "vS", ep_v5)
            mmc = 0
            for khalf in range(2):
                p.wait("sync", "tS", f2_tdone[khalf])
                srch = _setap(s_f2hi, [[D, 128], [128 * D, 16], [1, 128]],
                              offset=khalf * 2048 * D + oc * 128)
                srcl = _setap(s_f2lo, [[D, 128], [128 * D, 16], [1, 128]],
                              offset=khalf * 2048 * D + oc * 128)
                p.dma("sync", WH.ap()[:, khalf * 2048:khalf * 2048 + 2048], srch)
                p.dma("sync", WL.ap()[:, khalf * 2048:khalf * 2048 + 2048], srcl)
                p.qwait("tensor", "sync")
                for pas in range(3):
                    wsl = (WH if pas < 2 else WL)
                    for k2 in range(16):
                        kc = khalf * 16 + k2
                        mv = pa((0 if pas != 1 else 16384) + kc * 512, 512)
                        mmc = mm(bk, wsl.ap()[:, khalf * 2048 + k2 * 128:khalf * 2048 + k2 * 128 + 128],
                                 mv, start=(khalf == 0 and pas == 0 and k2 == 0),
                                 stop=(khalf == 1 and pas == 2 and k2 == 15))
                f2_tdone[khalf] = mmc
            p.wait("vector", "tS", mmc)
            ep_v5 = p.op("vector", "scalar_tensor_tensor",
                         X[oc].ap()[:, tc * 512:(tc + 1) * 512], bk, 1.0,
                         X[oc].ap()[:, tc * 512:(tc + 1) * 512],
                         op0=AluOpType.mult, op1=AluOpType.add)

    mark("outp")
    evac_prev2 = [0, 0]
    for t in range(NKT):
        par = t & 1
        p.wait("tensor", "vS", evac_prev2[par])
        if t == 0:
            p.wait_now("tensor", "vS")
        tcnt = 0
        for c in range(NCH):
            tcnt = p.op("tensor", "transpose",
                        PS.ap()[:, par * 1024 + c * 128:par * 1024 + c * 128 + 128],
                        X[c].ap()[:, t * 128:(t + 1) * 128], IDF.ap())
        p.wait("vector", "tS", tcnt)
        if t >= 2:
            p.qwait("vector", "gpsimd")
        ev = p.op("vector", "tensor_copy",
                  SC32.ap()[:, par * 1024:par * 1024 + 1024],
                  PS.ap()[:, par * 1024:par * 1024 + 1024])
        evac_prev2[par] = ev
        p.wait("gpsimd", "vS", ev)
        p.dma("gpsimd", out_hbm[t * 128:(t + 1) * 128, :],
              SC32.ap()[:, par * 1024:par * 1024 + 1024])

    p.finalize()
    return nc, di, dbg_t


def _build_consts():
    identf = np.eye(128, dtype=np.float32)
    identb = np.eye(128, dtype=ml_dtypes.bfloat16)
    onesb = np.ones((1, 128), dtype=ml_dtypes.bfloat16)
    j = np.arange(512)[None, :]
    pp = np.arange(128)[:, None]
    cm = np.zeros((128, 4 * 512), dtype=np.float32)
    for d in range(4):
        cm[:, d * 512:(d + 1) * 512] = np.where(pp + d * 128 > j, NEG, 0.0)
    cmask = cm.astype(ml_dtypes.bfloat16)
    proto = np.linspace(0.0, 1.0, 512).astype(np.float32)
    rel = (np.arange(T).astype(np.float32) / np.float32(T - 1))
    idx = np.clip(np.round(rel * np.float32(511)).astype(np.int32), 0, 511)
    KG = np.arange(101)[:, None]
    IDXg = idx[None, :]
    sel = IDXg - KG
    proto64 = np.linspace(0.0, 1.0, 512)
    pd64 = np.where(sel >= 0, proto64[np.clip(sel, 0, 511)], 0.0)
    protod = pd64.astype(np.float32)
    protod_lo = (pd64 - protod.astype(np.float64)).astype(np.float32)
    tie = (np.linspace(0.0, 1e-6, 16)[np.arange(T) % 16]).astype(ml_dtypes.bfloat16)[None, :]
    return dict(identf=identf, identb=identb, onesb=onesb, cmask=cmask,
                protod=protod, protod_lo=protod_lo, tie=tie)


def _prep_core(inputs, b, consts):
    m = dict(consts)
    m["x"] = np.ascontiguousarray(inputs["L0_attn_x"][b])
    m["w_dkv"] = inputs["L0_attn_W_DKV"]
    m["w_uk"] = inputs["L0_attn_W_UK"]
    m["w_uv"] = inputs["L0_attn_W_UV"]
    m["w_uq"] = inputs["L0_attn_W_UQ"]
    m["w_o"] = inputs["L0_attn_W_O"]
    m["w1t"] = np.ascontiguousarray(inputs["L0_ffn_w1_w"].T)
    m["w2t"] = np.ascontiguousarray(inputs["L0_ffn_w2_w"].T)
    m["h1t"] = np.ascontiguousarray(inputs["L0_ffn_h1_w"].T)
    m["h2t"] = np.ascontiguousarray(inputs["L0_ffn_h2_w"].T)
    m["w3t"] = np.ascontiguousarray(inputs["L0_ffn_w3_w"].T)
    parts = [inputs["L0_ffn_w1_s"], inputs["L0_ffn_w2_s"], inputs["L0_ffn_h1_s"],
             inputs["L0_ffn_h2_s"], inputs["L0_ffn_w3_s"],
             inputs["L1_ffn_bias"].reshape(-1)]
    flat = np.concatenate([np.asarray(x, np.float32).reshape(-1) for x in parts])
    flat = np.concatenate([flat, np.zeros(192 * 128 - flat.size, np.float32)])
    m["svec"] = np.ascontiguousarray(flat.reshape(192, 128).T)
    m["wqkvt"] = np.ascontiguousarray(inputs["L1_attn_Wqkv"].T)
    m["woutt"] = np.ascontiguousarray(inputs["L1_attn_Wout"].T)
    m["wposT"] = np.ascontiguousarray(inputs["L1_attn_Wpos"].T)
    m["coeffsT"] = np.ascontiguousarray(inputs["L1_attn_coeffs"].T)
    m["ctrl"] = np.ascontiguousarray(inputs["L1_ffn_controller"].reshape(D, 64))
    m["f1m"] = np.ascontiguousarray(inputs["L1_ffn_f1"].reshape(D, DH))
    m["f2m"] = np.ascontiguousarray(inputs["L1_ffn_f2"].reshape(DH, D))
    return m


def get_phases(nc_tuple=None):
    return None


def kernel(**inputs):
    consts = _build_consts()
    nc, di, dbg_t = build_nc()
    in_maps = [_prep_core(inputs, b, consts) for b in range(8)]
    res = run_bass_kernel_spmd(nc, in_maps, core_ids=list(range(8)))
    return np.stack([np.asarray(r["out"], np.float32) for r in res.results], axis=0)


# version 10 (final)
# speedup vs baseline: 1.0177x; 1.0177x over previous
import numpy as np
import ml_dtypes
DEBUG_BOUNDS = False
import concourse.bass as bass
import concourse.mybir as mybir
import bass_rust
from concourse.alu_op_type import AluOpType
from concourse.bass_utils import run_bass_kernel_spmd

F32 = mybir.dt.float32
FP16 = mybir.dt.float16
BF16 = mybir.dt.bfloat16
AF = mybir.ActivationFunctionType
AX = mybir.AxisListType

D = 1024
T = 2048
NH = 16
HD = 64
DL = 256
DH = 4096
NCH = 8
NTC = 4
NKT = 16
NEG = -1.0e30
EXPSHIFT = 56.0


def _vp(pairs):
    return bass_rust.VecI64Pair([list(x) for x in pairs])


def _setap(ap, pairs, offset=None):
    ap = ap.copy()
    ap.ap = _vp(pairs)
    if offset is not None:
        ap.offset = offset
    return ap


class Plan:

    ENGINES = ("sync", "gpsimd", "tensor", "vector", "scalar")
    QDEPTH = 8
    ESEM = {"tensor": "tS", "vector": "vS", "scalar": "sS"}

    def __init__(self, nc):
        self.nc = nc
        self.ops = {e: [] for e in self.ENGINES}
        self.sems = {}
        self.cnt = {}
        for name in ("tS", "vS", "sS"):
            self.sems[name] = nc.alloc_semaphore(name)
            self.cnt[name] = 0
        for q in ("sync", "gpsimd"):
            for i in range(self.QDEPTH):
                name = f"q_{q}_{i}"
                self.sems[name] = nc.alloc_semaphore(name)
                self.cnt[name] = 0
        self.qstate = {"sync": [0, False], "gpsimd": [0, False]}
        self.release = {f"q_{q}_{i}": [] for q in ("sync", "gpsimd")
                        for i in range(self.QDEPTH)}

    def emit(self, eng, fn):
        self.ops[eng].append(fn)

    def wait(self, eng, sem, val):
        if val <= 0:
            return
        if sem.startswith("q_") and DEBUG_BOUNDS:
            import traceback
            if val not in self.bounds.get(sem, set()) and val != self.cnt[sem]:
                print("NON-BOUNDARY WAIT", sem, val, "cnt", self.cnt[sem])
                traceback.print_stack()
        s = self.sems[sem]
        self.emit(eng, lambda e, s=s, val=val: e.wait_ge(s, val))

    def wait_now(self, eng, sem):
        self.wait(eng, sem, self.cnt[sem])

    def _qsem(self, qeng):
        return f"q_{qeng}_{self.qstate[qeng][0]}"

    def dma(self, qeng, out, in_, **kw):
        st = self.qstate[qeng]
        if not st[1]:
            st[0] = (st[0] + 1) % self.QDEPTH
            st[1] = True
            sem0 = self._qsem(qeng)
            rel = self.release[sem0]
            if any(r is None for r in rel):
                for es in ("tS", "vS", "sS"):
                    self.wait_now(qeng, es)
            else:
                full = False
                for es, target in rel:
                    if self.cnt[es] >= target:
                        self.wait(qeng, es, target)
                    else:
                        full = True
                if full:
                    for es in ("tS", "vS", "sS"):
                        self.wait_now(qeng, es)
            self.release[sem0] = []
        sem = self._qsem(qeng)
        s = self.sems[sem]
        self.cnt[sem] += 16
        self.emit(qeng, lambda e, o=out, i=in_, kw=kw, s=s: e.dma_start(out=o, in_=i, **kw).then_inc(s, 16))
        return sem, self.cnt[sem]

    def qwait(self, eng, qeng):
        st = self.qstate[qeng]
        sem = self._qsem(qeng)
        st[1] = False
        self.wait(eng, sem, self.cnt[sem])
        es = self.ESEM.get(eng)
        self.release[sem].append((es, self.cnt[es] + 1) if es else None)

    def qbarrier(self, eng, qeng):
        self.qstate[qeng][1] = False
        es = self.ESEM.get(eng)
        for i in range(self.QDEPTH):
            sem = f"q_{qeng}_{i}"
            if self.cnt[sem] > 0:
                self.wait(eng, sem, self.cnt[sem])
                self.release[sem].append((es, self.cnt[es] + 1) if es else None)

    def qclose(self, qeng):
        sem = self._qsem(qeng)
        self.qstate[qeng][1] = False
        return (sem, self.cnt[sem])

    def qwait_tok(self, eng, tok):
        sem, val = tok
        if val <= 0:
            return
        if self.cnt[sem] > val:
            for q, st in self.qstate.items():
                if f"q_{q}_{st[0]}" == sem:
                    st[1] = False
            val = self.cnt[sem]
        self.wait(eng, sem, val)
        es = self.ESEM.get(eng)
        self.release[sem].append((es, self.cnt[es] + 1) if es else None)

    def op(self, eng, method, *args, **kw):
        sem = {"tensor": "tS", "vector": "vS", "scalar": "sS"}[eng]
        s = self.sems[sem]
        self.cnt[sem] += 1
        def fn(e, method=method, args=args, kw=kw, s=s):
            getattr(e, method)(*args, **kw).then_inc(s, 1)
        self.emit(eng, fn)
        return self.cnt[sem]

    def finalize(self):
        if hasattr(self, "phases"):
            self.nc._phases = list(self.phases) + [("end", self.cnt["tS"])]
        for q in ("sync", "gpsimd"):
            for i in range(self.QDEPTH):
                sem = f"q_{q}_{i}"
                if self.cnt[sem] > 0:
                    self.wait(q, sem, self.cnt[sem])
        with self.nc.Block() as block:
            for eng in self.ENGINES:
                ops = self.ops[eng]
                def make_body(ops):
                    def body(e):
                        for fn in ops:
                            fn(e)
                    return body
                getattr(block, eng)(make_body(ops))


def build_nc(dbg=(), upto=99):
    dbg = set(dbg)
    nc = bass.Bass("TRN2", target_bir_lowering=False)

    di = {}
    def dram_in(name, shape, dt=F32):
        di[name] = nc.dram_tensor(name, list(shape), dt, kind="ExternalInput").ap()
        return di[name]

    x_in = dram_in("x", (T, D))
    w_dkv = dram_in("w_dkv", (D, DL))
    w_uk = dram_in("w_uk", (DL, D))
    w_uv = dram_in("w_uv", (DL, D))
    w_uq = dram_in("w_uq", (D, D))
    w_o = dram_in("w_o", (D, D))
    w1t = dram_in("w1t", (D, DH))
    w2t = dram_in("w2t", (D, DH))
    h1t = dram_in("h1t", (DH, DH))
    h2t = dram_in("h2t", (DH, DH))
    w3t = dram_in("w3t", (DH, D))
    svec_in = dram_in("svec", (128, 192))
    wqkvt = dram_in("wqkvt", (D, 3 * D))
    woutt = dram_in("woutt", (D, D))
    wposT = dram_in("wposT", (4, NH))
    coeffsT = dram_in("coeffsT", (101, 4))
    ctrl = dram_in("ctrl", (D, 64))
    f1m = dram_in("f1m", (D, DH))
    f2m = dram_in("f2m", (DH, D))
    identf = dram_in("identf", (128, 128))
    identb = dram_in("identb", (128, 128), BF16)
    onesb = dram_in("onesb", (1, 128), BF16)
    protod_in = dram_in("protod", (101, T))
    protod_lo_in = dram_in("protod_lo", (101, T))
    cmask_in = dram_in("cmask", (128, 4 * 512), BF16)
    tie_in = dram_in("tie", (1, T), BF16)

    out_hbm = nc.dram_tensor("out", [T, D], F32, kind="ExternalOutput").ap()

    dbg_t = {}
    def dbg_out(name, shape, dt=FP16):
        dbg_t[name] = nc.dram_tensor("dbg_" + name, list(shape), dt,
                                     kind="ExternalOutput").ap()
        return dbg_t[name]
    if "xn0" in dbg:
        dbg_out("xn0", (D, 2 * T))
    if "ckv" in dbg:
        dbg_out("ckv", (DL, 2 * T))
    if "kvq" in dbg:
        for n in ("khi", "klo", "qhi", "qlo"):
            dbg_out(n, (D, T))
        for n in ("vhi", "vlo"):
            dbg_out(n, (T, D))
    if "affn" in dbg:
        dbg_out("ahi", (DH, T)); dbg_out("alo", (DH, T))
    if "ctx0" in dbg:
        dbg_out("ctxhi", (D, T)); dbg_out("ctxlo", (D, T))
    if "x1a" in dbg:
        dbg_out("x1a", (D, T), F32)
    if "x1" in dbg:
        dbg_out("x1", (D, T), F32)
    if "gsh" in dbg:
        dbg_out("gsh", (16, 4224))
    if "xn1" in dbg:
        dbg_out("xn1", (D, 2 * T))
    if "kvq1" in dbg:
        for n in ("khi", "klo", "qhi", "qlo"):
            dbg_out(n, (D, T))
        for n in ("vhi", "vlo"):
            dbg_out(n, (T, D))
    if "perm" in dbg:
        dbg_out("perm", (64, T))
        dbg_out("lg", (64, T), F32)
    if "ym" in dbg:
        dbg_out("ymhi", (DH, T)); dbg_out("ymlo", (DH, T))
    if "x2a" in dbg:
        dbg_out("x2a", (D, T), F32)
    if "vslab" in dbg:
        dbg_out("vslabh", (128, 4096))
        dbg_out("vslabl", (128, 4096))

    def scratch(name, shape, dt=FP16):
        return nc.dram_tensor(name, list(shape), dt, kind="Internal").ap()

    s_khi = scratch("s_khi", (D, T)); s_klo = scratch("s_klo", (D, T))
    s_qhi = scratch("s_qhi", (D, T)); s_qlo = scratch("s_qlo", (D, T))
    s_vhi = scratch("s_vhi", (T, D)); s_vlo = scratch("s_vlo", (T, D))
    s_ahi = scratch("s_ahi", (DH, T)); s_alo = scratch("s_alo", (DH, T))
    s_bhi = scratch("s_bhi", (DH, T)); s_blo = scratch("s_blo", (DH, T))
    s_ctxhi = scratch("s_ctxhi", (D, T)); s_ctxlo = scratch("s_ctxlo", (D, T))
    s_h1 = scratch("s_h1", (DH, DH)); s_h2 = scratch("s_h2", (DH, DH))
    s_w3 = scratch("s_w3", (DH, D))
    s_f2hi = scratch("s_f2hi", (DH, D)); s_f2lo = scratch("s_f2lo", (DH, D))
    s_gsh = [scratch(f"s_gsh{i}", (128 * 4224,), FP16) for i in range(2)]

    X = [nc.alloc_sbuf_tensor(f"X{c}", [128, T], F32) for c in range(NCH)]
    PA = nc.alloc_sbuf_tensor("PA", [128, 32768], FP16)
    CKV = nc.alloc_sbuf_tensor("CKV", [128, 8192], FP16)
    WS32 = nc.alloc_sbuf_tensor("WS32", [128, 2048], F32)
    WH = nc.alloc_sbuf_tensor("WH", [128, 4096], FP16)
    WL = nc.alloc_sbuf_tensor("WL", [128, 4096], FP16)
    SC32 = nc.alloc_sbuf_tensor("SC32", [128, 2048], F32)
    MISC = nc.alloc_sbuf_tensor("MISC", [128, 2048], F32)
    GSH1 = nc.alloc_sbuf_tensor("GSH1", [128, 4224], FP16)
    IDF = nc.alloc_sbuf_tensor("IDF", [128, 128], F32)
    IDB = nc.alloc_sbuf_tensor("IDB", [128, 128], BF16)
    IDH = nc.alloc_sbuf_tensor("IDH", [128, 128], FP16)
    ONE1 = nc.alloc_sbuf_tensor("ONE1", [1, 128], BF16)
    ONESC = nc.alloc_sbuf_tensor("ONESC", [128, 1], F32)
    EPS = nc.alloc_sbuf_tensor("EPS", [128, 1], F32)
    CMSK = nc.alloc_sbuf_tensor("CMSK", [128, 2048], BF16)
    SVEC = nc.alloc_sbuf_tensor("SVEC", [128, 192], F32)
    PS = nc.alloc_psum_tensor("PS", [128, 4096], F32)

    p = Plan(nc)
    p.phases = []
    def mark(name):
        p.phases.append((name, p.cnt["tS"]))

    def bank(i, r0=0, r1=128, w=512):
        return PS.ap()[r0:r1, i * 512:i * 512 + w]

    def pa(off, w, r0=0, r1=128):
        return PA.ap()[r0:r1, off:off + w]

    def ckv_ap(off, w, r0=0, r1=128):
        return CKV.ap()[r0:r1, off:off + w]

    def mm(bk, lhsT, rhs, start, stop, tp=None):
        return p.op("tensor", "matmul", bk, lhsT, rhs, start=start, stop=stop,
                    tile_position=tp)

    def bcast_row(dst_ap, src_row_ap, width, q="gpsimd"):
        pitch = src_row_ap.ap[0][0]
        src = _setap(src_row_ap, [[pitch, 1], [0, dst_ap.shape[0]], [1, width]])
        return p.dma(q, dst_ap, src)

    p.dma("sync", IDF.ap(), identf[:, :])
    p.dma("sync", IDB.ap(), identb[:, :])
    p.dma("sync", ONE1.ap(), onesb[:, :])
    p.dma("sync", CMSK.ap(), cmask_in[:, :])
    p.dma("sync", SVEC.ap(), svec_in[:, :])
    p.qwait("vector", "sync")
    v_ones = p.op("vector", "memset", ONESC.ap(), 1.0)
    p.op("vector", "memset", EPS.ap(), 1e-5)
    p.op("vector", "tensor_copy", IDH.ap(), IDF.ap())

    mark("xload")
    ld_prev = 0
    evac_prev = [0, 0]
    for t in range(NKT):
        par = t & 1
        stg = SC32.ap()[:, par * 1024:par * 1024 + 1024]
        if t >= 2:
            p.wait("sync", "vS", evac_prev[par])
        p.dma("sync", stg, x_in[t * 128:(t + 1) * 128, :])
        p.qwait("tensor", "sync")
        if t >= 2:
            p.wait("tensor", "vS", evac_prev[par])
        tcnt = 0
        for c in range(NCH):
            tcnt = p.op("tensor", "transpose", PS.ap()[:, par * 1024 + c * 128: par * 1024 + c * 128 + 128],
                        stg[:, c * 128:(c + 1) * 128], IDF.ap())
        p.wait("vector", "tS", tcnt)
        for c in range(NCH):
            ev = p.op("vector", "tensor_copy", X[c].ap()[:, t * 128:(t + 1) * 128],
                      PS.ap()[:, par * 1024 + c * 128: par * 1024 + c * 128 + 128])
        evac_prev[par] = ev

    def ln_phase(tag):
        sq_cnt = 0
        mm_cnt = 0
        for c in range(NCH):
            p.wait("scalar", "tS", mm_cnt)
            p.wait("scalar", "vS", p.cnt["vS"])
            sq_cnt = p.op("scalar", "activation", SC32.ap(), X[c].ap(), AF.Square,
                          scale=1.0)
            p.wait("tensor", "sS", sq_cnt)
            for tc in range(NTC):
                mm(bank(tc, 0, 1), ONESC.ap(), X[c].ap()[:, tc * 512:(tc + 1) * 512],
                   start=(c == 0), stop=(c == NCH - 1))
                mm_cnt = mm(bank(4 + tc, 0, 1), ONESC.ap(), SC32.ap()[:, tc * 512:(tc + 1) * 512],
                            start=(c == 0), stop=(c == NCH - 1))
        p.wait("vector", "tS", mm_cnt)
        p.wait_now("vector", "sS")
        for tc in range(NTC):
            p.op("vector", "tensor_scalar", MISC.ap()[0:1, tc * 512:(tc + 1) * 512],
                 bank(tc, 0, 1), 1.0 / D, None, op0=AluOpType.mult)
            p.op("vector", "tensor_scalar", SC32.ap()[0:1, tc * 512:(tc + 1) * 512],
                 bank(4 + tc, 0, 1), 1.0 / D, None, op0=AluOpType.mult)
        p.wait_now("vector", "vS")
        p.op("vector", "scalar_tensor_tensor", WS32.ap()[0:1, 0:2048], MISC.ap()[0:1, :],
             1.0, MISC.ap()[0:1, :], op0=AluOpType.mult, op1=AluOpType.mult)
        p.wait_now("vector", "vS")
        vcnt = p.op("vector", "tensor_tensor", PS.ap()[0:1, 0:2048], SC32.ap()[0:1, :],
                    WS32.ap()[0:1, 0:2048], op=AluOpType.subtract)
        p.wait("scalar", "vS", vcnt)
        scnt = p.op("scalar", "activation", PS.ap()[0:1, 2048:4096], PS.ap()[0:1, 0:2048],
                    AF.Sqrt, bias=EPS.ap()[0:1, :], scale=1.0)
        p.wait("vector", "sS", scnt)
        vcnt2 = p.op("vector", "reciprocal", MISC.ap()[32:33, 0:2048], PS.ap()[0:1, 2048:4096])
        p.wait("gpsimd", "vS", vcnt2)
        p.wait_now("gpsimd", "sS")
        c_mb = bcast_row(WS32.ap()[:, 0:2048], MISC.ap()[0:1, 0:2048], 2048)
        c_rb = bcast_row(SC32.ap()[:, 1024:2048], MISC.ap()[32:33, 0:1024], 1024)
        c_rb2 = bcast_row(SC32.ap()[:, 0:1024], MISC.ap()[32:33, 1024:2048], 1024)
        p.qwait("vector", "gpsimd")
        for c in range(NCH):
            for h in range(2):
                xsl = X[c].ap()[:, h * 1024:(h + 1) * 1024]
                msl = WS32.ap()[:, h * 1024:(h + 1) * 1024]
                rsl = SC32.ap()[:, 1024:2048] if h == 0 else SC32.ap()[:, 0:1024]
                t1 = PS.ap()[:, 0:1024] if (c & 1) == 0 else PS.ap()[:, 1024:2048]
                xn32 = PS.ap()[:, 2048:3072] if (c & 1) == 0 else PS.ap()[:, 3072:4096]
                p.wait_now("vector", "vS")
                p.op("vector", "tensor_tensor", t1, xsl, msl, op=AluOpType.subtract)
                p.wait_now("vector", "vS")
                vc = p.op("vector", "tensor_tensor", xn32, t1, rsl, op=AluOpType.mult)
                p.wait("scalar", "vS", vc)
                sc = p.op("scalar", "copy", pa(c * 2048 + h * 1024, 1024), xn32)
                p.wait("vector", "sS", sc)
                p.op("vector", "scalar_tensor_tensor",
                     pa(16384 + c * 2048 + h * 1024, 1024), xn32, 1.0,
                     pa(c * 2048 + h * 1024, 1024),
                     op0=AluOpType.mult, op1=AluOpType.subtract)
        return

    def stage_general(wsrc, rows, col0, ncols, slot, row0=0):
        nk = rows // 128
        w = nk * ncols
        wtot = wsrc.shape[1]
        src = _setap(wsrc, [[wtot, 128], [128 * wtot, nk], [1, ncols]],
                     offset=row0 * wtot + col0)
        p.wait_now("sync", "sS")
        p.wait_now("sync", "vS")
        p.dma("sync", WS32.ap()[:, 0:w], src)
        p.qwait("scalar", "sync")
        p.wait_now("scalar", "tS")
        p.wait_now("scalar", "vS")
        p.qbarrier("scalar", "gpsimd")
        scnt = p.op("scalar", "copy", WH.ap()[:, slot:slot + w], WS32.ap()[:, 0:w])
        p.wait("vector", "sS", scnt)
        p.qbarrier("vector", "gpsimd")
        vcnt = p.op("vector", "scalar_tensor_tensor", WL.ap()[:, slot:slot + w],
                    WS32.ap()[:, 0:w], 1.0, WH.ap()[:, slot:slot + w],
                    op0=AluOpType.mult, op1=AluOpType.subtract)
        return scnt, vcnt

    def stage_ternary_f32(wsrc, rows, col0, ncols, slot):
        nk = rows // 128
        w = nk * ncols
        wtot = wsrc.shape[1]
        src = _setap(wsrc, [[wtot, 128], [128 * wtot, nk], [1, ncols]], offset=col0)
        p.wait_now("sync", "sS")
        p.dma("sync", WS32.ap()[:, 0:w], src)
        p.qwait("scalar", "sync")
        p.wait_now("scalar", "tS")
        p.wait_now("scalar", "vS")
        p.qbarrier("scalar", "gpsimd")
        scnt = p.op("scalar", "copy", WH.ap()[:, slot:slot + w], WS32.ap()[:, 0:w])
        return scnt

    class Stager:

        def __init__(self, halfw=1024):
            self.halfw = halfw
            self.tdone = [p.cnt["tS"], p.cnt["tS"]]
            self.ws = [(p.cnt["sS"], p.cnt["vS"])] * 2
            self.i = 0
            self.par = 0

        def stage(self, wsrc, rows, col0, ncols, row0=0, ternary=False):
            par = self.i & 1
            self.par = par
            self.i += 1
            slot = par * self.halfw
            nk = rows // 128
            w = nk * ncols
            wtot = wsrc.shape[1]
            srcap = _setap(wsrc, [[wtot, 128], [128 * wtot, nk], [1, ncols]],
                           offset=row0 * wtot + col0)
            ps, pv = self.ws[par]
            p.wait("sync", "sS", ps)
            p.wait("sync", "vS", pv)
            p.dma("sync", WS32.ap()[:, par * 1024:par * 1024 + w], srcap)
            p.qwait("scalar", "sync")
            p.wait("scalar", "tS", self.tdone[par])
            scnt = p.op("scalar", "copy", WH.ap()[:, slot:slot + w],
                        WS32.ap()[:, par * 1024:par * 1024 + w])
            vcnt = 0
            if not ternary:
                p.wait("vector", "sS", scnt)
                p.wait("vector", "tS", self.tdone[par])
                vcnt = p.op("vector", "scalar_tensor_tensor", WL.ap()[:, slot:slot + w],
                            WS32.ap()[:, par * 1024:par * 1024 + w], 1.0,
                            WH.ap()[:, slot:slot + w],
                            op0=AluOpType.mult, op1=AluOpType.subtract)
            self.ws[par] = (scnt, vcnt)
            p.wait("tensor", "sS", scnt)
            if not ternary:
                p.wait("tensor", "vS", vcnt)
            return slot

        def done(self, mmcount):
            self.tdone[self.par] = mmcount

    class SpillSlots:

        def __init__(self, hi_buf, lo_buf, offs, rows=128):
            self.hi_buf, self.lo_buf = hi_buf, lo_buf
            self.offs = offs
            self.tok = [None] * len(offs)
            self.i = 0
            self.rows = rows

        def spill(self, src32, hi_dst, lo_dst):
            par = self.i % len(self.offs)
            self.i += 1
            ho, lo_ = self.offs[par]
            hi_st = self.hi_buf.ap()[0:self.rows, ho:ho + 512]
            lo_st = self.lo_buf.ap()[0:self.rows, lo_:lo_ + 512]
            if self.tok[par] is not None:
                p.qwait_tok("scalar", self.tok[par])
                p.qwait_tok("vector", self.tok[par])
            else:
                p.qbarrier("scalar", "gpsimd")
                p.qbarrier("vector", "gpsimd")
            p.wait_now("scalar", "sS")
            sc2 = p.op("scalar", "copy", hi_st, src32)
            p.wait("vector", "sS", sc2)
            vc2 = p.op("vector", "scalar_tensor_tensor", lo_st, src32, 1.0, hi_st,
                       op0=AluOpType.mult, op1=AluOpType.subtract)
            p.wait("gpsimd", "sS", sc2)
            p.wait("gpsimd", "vS", vc2)
            p.dma("gpsimd", hi_dst, hi_st)
            p.dma("gpsimd", lo_dst, lo_st)
            self.tok[par] = p.qclose("gpsimd")
            return sc2, vc2

    def xn_hi(c, tc=None, w=512):
        if tc is None:
            return pa(c * 2048, 2048)
        return pa(c * 2048 + tc * 512, w)

    def xn_lo(c, tc=None, w=512):
        if tc is None:
            return pa(16384 + c * 2048, 2048)
        return pa(16384 + c * 2048 + tc * 512, w)

    mark("ln0")
    ln_phase("ln0")
    if "xn0" in dbg:
        p.wait_now("gpsimd", "vS")
        p.wait_now("gpsimd", "sS")
        for c in range(NCH):
            p.dma("gpsimd", dbg_t["xn0"][c * 128:(c + 1) * 128, 0:T], xn_hi(c))
            p.dma("gpsimd", dbg_t["xn0"][c * 128:(c + 1) * 128, T:2 * T], xn_lo(c))
    if upto <= 0:
        p.finalize()
        return nc, di, dbg_t

    mark("ckv")
    for oc in range(2):
        sc, vc = stage_general(w_dkv, D, oc * 128, 128, slot=0)
        p.wait("tensor", "sS", sc)
        p.wait("tensor", "vS", vc)
        mmc = 0
        for pas in range(3):
            for kc in range(NCH):
                st = WH.ap()[:, kc * 128:(kc + 1) * 128] if pas < 2 else WL.ap()[:, kc * 128:(kc + 1) * 128]
                for tc in range(NTC):
                    mv = xn_hi(kc, tc) if pas != 1 else xn_lo(kc, tc)
                    mmc = mm(bank(tc), st, mv, start=(pas == 0 and kc == 0),
                             stop=(pas == 2 and kc == NCH - 1))
        p.wait("scalar", "tS", mmc)
        for tc in range(NTC):
            sc2 = p.op("scalar", "copy", ckv_ap(oc * 2048 + tc * 512, 512), bank(tc))
            p.wait("vector", "sS", sc2)
            p.op("vector", "scalar_tensor_tensor", ckv_ap(4096 + oc * 2048 + tc * 512, 512),
                 bank(tc), 1.0, ckv_ap(oc * 2048 + tc * 512, 512),
                 op0=AluOpType.mult, op1=AluOpType.subtract)
        p.wait_now("tensor", "vS")
    if "ckv" in dbg:
        p.wait_now("gpsimd", "vS")
        p.wait_now("gpsimd", "sS")
        for oc in range(2):
            p.dma("gpsimd", dbg_t["ckv"][oc * 128:(oc + 1) * 128, 0:T], ckv_ap(oc * 2048, 2048))
            p.dma("gpsimd", dbg_t["ckv"][oc * 128:(oc + 1) * 128, T:2 * T], ckv_ap(4096 + oc * 2048, 2048))
    if upto <= 1:
        p.finalize()
        return nc, di, dbg_t

    def ckv_hi(kc, tc=None, w=512):
        if tc is None:
            return ckv_ap(kc * 2048, 2048)
        return ckv_ap(kc * 2048 + tc * 512, w)

    def ckv_lo(kc, tc=None, w=512):
        if tc is None:
            return ckv_ap(4096 + kc * 2048, 2048)
        return ckv_ap(4096 + kc * 2048 + tc * 512, w)

    def spill_split(bk_ap, hi_dst, lo_dst, sc_off):
        hi_t = WH.ap()[:, 2048 + sc_off:2048 + sc_off + 512]
        lo_t = WL.ap()[:, 2048 + sc_off:2048 + sc_off + 512]
        sc2 = p.op("scalar", "copy", hi_t, bk_ap)
        p.wait("vector", "sS", sc2)
        vc2 = p.op("vector", "scalar_tensor_tensor", lo_t, bk_ap, 1.0, hi_t,
                   op0=AluOpType.mult, op1=AluOpType.subtract)
        p.wait("gpsimd", "sS", sc2)
        p.wait("gpsimd", "vS", vc2)
        p.dma("gpsimd", hi_dst, hi_t)
        return p.dma("gpsimd", lo_dst, lo_t)

    def proj_general(wsrc, d_out, mov_hi, mov_lo, nkc, spill_to, qkv_route=None):
        stg = Stager()
        spl = SpillSlots(WH, WL, [(2048, 2048), (2560, 2560), (3072, 3072), (3584, 3584)])
        epb = {0: (0, 0), 1: (0, 0)}
        for oc in range(d_out // 128):
            slot = stg.stage(wsrc, nkc * 128, oc * 128, 128)
            bpar = oc & 1
            p.wait("tensor", "sS", epb[bpar][0])
            p.wait("tensor", "vS", epb[bpar][1])
            mmc = 0
            for pas in range(3):
                for kc in range(nkc):
                    st = (WH if pas < 2 else WL).ap()[:, slot + kc * 128:slot + kc * 128 + 128]
                    for tc in range(NTC):
                        mv = mov_hi(kc, tc) if pas != 1 else mov_lo(kc, tc)
                        mmc = mm(bank(bpar * 4 + tc), st, mv, start=(pas == 0 and kc == 0),
                                 stop=(pas == 2 and kc == nkc - 1))
            stg.done(mmc)
            p.wait("scalar", "tS", mmc)
            p.wait("vector", "tS", mmc)
            dh_, dl_, r = (spill_to[0], spill_to[1], oc * 128) if qkv_route is None else qkv_route(oc)
            for tc in range(NTC):
                es2, ev2 = spl.spill(bank(bpar * 4 + tc),
                                     dh_[r:r + 128, tc * 512:(tc + 1) * 512],
                                     dl_[r:r + 128, tc * 512:(tc + 1) * 512])
            epb[bpar] = (es2, ev2)

    mark("kproj")
    proj_general(w_uk, D, ckv_hi, ckv_lo, 2, (s_khi, s_klo))

    mark("vproj")
    sc, vc = stage_general(w_uv, DL, 0, D, slot=0)
    p.wait("tensor", "sS", sc)
    p.wait("tensor", "vS", vc)
    spv = SpillSlots(WH, WL, [(2048, 2048), (2560, 2560), (3072, 3072), (3584, 3584)])
    epv = {0: (0, 0), 1: (0, 0)}
    for tt in range(NKT):
        bpar = tt & 1
        p.wait("tensor", "sS", epv[bpar][0])
        p.wait("tensor", "vS", epv[bpar][1])
        mmc = 0
        for dvc in range(2):
            bk = bank(bpar * 2 + dvc)
            for pas in range(3):
                for kc in range(2):
                    if pas == 0:
                        st, mvw = ckv_hi(kc, None)[:, tt * 128:(tt + 1) * 128], WH
                    elif pas == 1:
                        st, mvw = ckv_lo(kc, None)[:, tt * 128:(tt + 1) * 128], WH
                    else:
                        st, mvw = ckv_hi(kc, None)[:, tt * 128:(tt + 1) * 128], WL
                    mv = mvw.ap()[:, kc * 1024 + dvc * 512: kc * 1024 + dvc * 512 + 512]
                    mmc = mm(bk, st, mv, start=(pas == 0 and kc == 0),
                             stop=(pas == 2 and kc == 1))
        p.wait("scalar", "tS", mmc)
        p.wait("vector", "tS", mmc)
        for dvc in range(2):
            es2, ev2 = spv.spill(bank(bpar * 2 + dvc),
                                 s_vhi[tt * 128:(tt + 1) * 128, dvc * 512:(dvc + 1) * 512],
                                 s_vlo[tt * 128:(tt + 1) * 128, dvc * 512:(dvc + 1) * 512])
        epv[bpar] = (es2, ev2)

    mark("qproj")
    proj_general(w_uq, D, xn_hi, xn_lo, NCH, (s_qhi, s_qlo))

    if "kvq" in dbg:
        p.qbarrier("sync", "gpsimd")
        p.dma("sync", dbg_t["khi"][:, :], s_khi[:, :])
        p.dma("sync", dbg_t["klo"][:, :], s_klo[:, :])
        p.dma("sync", dbg_t["qhi"][:, :], s_qhi[:, :])
        p.dma("sync", dbg_t["qlo"][:, :], s_qlo[:, :])
        p.dma("sync", dbg_t["vhi"][:, :], s_vhi[:, :])
        p.dma("sync", dbg_t["vlo"][:, :], s_vlo[:, :])
    if upto <= 2:
        p.finalize()
        return nc, di, dbg_t

    def spill_split2(src32, hi_dst, lo_dst, hi_st, lo_st, w=512):
        p.wait_now("scalar", "sS")
        sc2 = p.op("scalar", "copy", hi_st, src32)
        p.wait("vector", "sS", sc2)
        vc2 = p.op("vector", "scalar_tensor_tensor", lo_st, src32, 1.0, hi_st,
                   op0=AluOpType.mult, op1=AluOpType.subtract)
        p.wait("gpsimd", "sS", sc2)
        p.wait("gpsimd", "vS", vc2)
        p.dma("gpsimd", hi_dst, hi_st)
        p.dma("gpsimd", lo_dst, lo_st)
        return sc2, vc2

    mark("s1")
    conv_work = []
    conv_rel = [0, 0]

    def convert_fp16(src, dst, rows, cols):
        cw = min(2048, cols)
        for r in range(rows // 128):
            for half in range(cols // cw):
                def item(src=src, dst=dst, r=r, half=half, cw=cw):
                    p.wait("sync", "sS", conv_rel[0])
                    p.wait("sync", "vS", conv_rel[1])
                    p.dma("sync", WS32.ap()[:, 0:cw], src[r * 128:(r + 1) * 128, half * cw:(half + 1) * cw])
                    p.qwait("scalar", "sync")
                    p.qwait("scalar", "gpsimd")
                    sc = p.op("scalar", "copy", WL.ap()[:, 2048:2048 + cw], WS32.ap()[:, 0:cw])
                    conv_rel[0] = sc
                    p.wait("gpsimd", "sS", sc)
                    p.dma("gpsimd", dst[r * 128:(r + 1) * 128, half * cw:(half + 1) * cw],
                          WL.ap()[:, 2048:2048 + cw])
                conv_work.append(item)

    conv_done = [0]

    def pump(n, drain=True):
        n = min(n, len(conv_work))
        if n == 0:
            return
        if drain:
            p.wait_now("sync", "sS")
            p.wait_now("sync", "vS")
            conv_rel[0] = max(conv_rel[0], p.cnt["sS"])
            conv_rel[1] = max(conv_rel[1], p.cnt["vS"])
        for _ in range(n):
            conv_work.pop(0)()
            conv_done[0] += 1

    def pump_all():
        pump(len(conv_work))

    def convert_split_fp16(src, dsthi, dstlo, rows, cols):
        cw = min(2048, cols)
        for r in range(rows // 128):
            for half in range(cols // cw):
                def item(src=src, dsthi=dsthi, dstlo=dstlo, r=r, half=half, cw=cw):
                    assert cw <= 1024
                    p.wait("sync", "sS", conv_rel[0])
                    p.wait("sync", "vS", conv_rel[1])
                    p.dma("sync", WS32.ap()[:, 0:cw], src[r * 128:(r + 1) * 128, half * cw:(half + 1) * cw])
                    p.qwait("scalar", "sync")
                    p.qwait("scalar", "gpsimd")
                    sc = p.op("scalar", "copy", WL.ap()[:, 2048:2048 + cw], WS32.ap()[:, 0:cw])
                    p.wait("vector", "sS", sc)
                    p.qwait("vector", "gpsimd")
                    vc = p.op("vector", "scalar_tensor_tensor", WL.ap()[:, 2048 + cw:2048 + 2 * cw],
                              WS32.ap()[:, 0:cw], 1.0, WL.ap()[:, 2048:2048 + cw],
                              op0=AluOpType.mult, op1=AluOpType.subtract)
                    conv_rel[0] = sc
                    conv_rel[1] = vc
                    p.wait("gpsimd", "sS", sc)
                    p.wait("gpsimd", "vS", vc)
                    p.dma("gpsimd", dsthi[r * 128:(r + 1) * 128, half * cw:(half + 1) * cw],
                          WL.ap()[:, 2048:2048 + cw])
                    p.dma("gpsimd", dstlo[r * 128:(r + 1) * 128, half * cw:(half + 1) * cw],
                          WL.ap()[:, 2048 + cw:2048 + 2 * cw])
                conv_work.append(item)

    ep = {0: (0, 0), 1: (0, 0)}
    s1_tdone = [p.cnt["tS"], p.cnt["tS"]]
    s1_ws = [p.cnt["sS"], p.cnt["sS"]]
    sp1 = SpillSlots(WL, WL, [(0, 1024), (512, 1536)])
    for dh in range(32):
        par = dh & 1
        base = par * 2048
        scl = 0
        for j, wt_ in enumerate((w1t, w2t)):
            srcap = _setap(wt_, [[DH, 128], [128 * DH, NCH], [1, 128]], offset=dh * 128)
            p.wait("sync", "sS", s1_ws[j])
            p.dma("sync", WS32.ap()[:, j * 1024:(j + 1) * 1024], srcap)
            p.qwait("scalar", "sync")
            p.wait("scalar", "tS", s1_tdone[par])
            scl = p.op("scalar", "copy", WH.ap()[:, base + j * 1024:base + (j + 1) * 1024],
                       WS32.ap()[:, j * 1024:(j + 1) * 1024])
            s1_ws[j] = scl
        p.wait("tensor", "sS", scl)
        b1, b2 = bank(par * 2), bank(par * 2 + 1)
        for tc in range(NTC):
            mmc = 0
            p.wait("tensor", "sS", ep[par][0])
            p.wait("tensor", "vS", ep[par][1])
            for pas in range(2):
                for kc in range(NCH):
                    mv = xn_hi(kc, tc) if pas == 0 else xn_lo(kc, tc)
                    mm(b1, WH.ap()[:, base + kc * 128:base + kc * 128 + 128], mv,
                       start=(pas == 0 and kc == 0), stop=(pas == 1 and kc == NCH - 1))
                    mmc = mm(b2, WH.ap()[:, base + 1024 + kc * 128:base + 1024 + kc * 128 + 128], mv,
                             start=(pas == 0 and kc == 0), stop=(pas == 1 and kc == NCH - 1))
            s1_tdone[par] = mmc
            p.wait("scalar", "tS", mmc)
            u = SC32.ap()[:, par * 1024:par * 1024 + 512]
            v = SC32.ap()[:, par * 1024 + 512:par * 1024 + 1024]
            a32 = MISC.ap()[:, par * 1024:par * 1024 + 512]
            p.wait_now("scalar", "sS")
            su = p.op("scalar", "activation", u, b1, AF.Silu,
                      scale=SVEC.ap()[:, dh:dh + 1])
            sv = p.op("scalar", "activation", v, b2, AF.Copy,
                      scale=SVEC.ap()[:, 32 + dh:32 + dh + 1])
            p.wait("vector", "sS", sv)
            p.wait_now("vector", "vS")
            va = p.op("vector", "tensor_tensor", a32, u, v, op=AluOpType.mult)
            p.wait("scalar", "vS", va)
            es, ev = sp1.spill(a32,
                               s_ahi[dh * 128:(dh + 1) * 128, tc * 512:(tc + 1) * 512],
                               s_alo[dh * 128:(dh + 1) * 128, tc * 512:(tc + 1) * 512])
            ep[par] = (es, ev)
    if "affn" in dbg:
        p.qbarrier("sync", "gpsimd")
        p.dma("sync", dbg_t["ahi"][:, :], s_ahi[:, :])
        p.dma("sync", dbg_t["alo"][:, :], s_alo[:, :])
    if upto <= 3:
        p.finalize()
        return nc, di, dbg_t

    def attn_core(layer, pump_sched=()):
        ep_bank = {i: (0, 0) for i in range(4)}
        ep_ctx = {}
        avdone = {}
        ep_p32 = {}
        spc = SpillSlots(CKV, CKV, [(4224, 4736), (5248, 5760)], rows=64)

        def phi_ap(h2, par):
            return pa(8192 + (h2 * 4 + par) * 512, 512)

        def plo_ap(h2, par):
            return pa(20736 + (h2 * 4 + par) * 512, 512)

        def gsh_ap(h2, off, w=512):
            return pa(12288 + h2 * 4224 + off, w)

        VREG1 = 24832

        def vav_st(reg, off, w):
            return ckv_ap(off, w) if reg == 0 else pa(VREG1 + off, w)

        def v_load(hp2, reg):
            for h2 in range(2):
                h = 2 * hp2 + h2
                vsrc_hi = _setap(s_vhi, [[1024, 128], [128 * 1024, NKT], [1, 64]],
                                 offset=64 * h)
                vsrc_lo = _setap(s_vlo, [[1024, 128], [128 * 1024, NKT], [1, 64]],
                                 offset=64 * h)
                if reg == 0:
                    dst_hi = _setap(CKV.ap(), [[8192, 128], [66, NKT], [1, 64]],
                                    offset=h2 * 2112)
                    dst_lo = _setap(CKV.ap(), [[8192, 128], [66, NKT], [1, 64]],
                                    offset=h2 * 2112 + 1056)
                else:
                    dst_hi = _setap(PA.ap(), [[32768, 128], [66, NKT], [1, 64]],
                                    offset=VREG1 + h2 * 2112)
                    dst_lo = _setap(PA.ap(), [[32768, 128], [66, NKT], [1, 64]],
                                    offset=VREG1 + h2 * 2112 + 1056)
                p.dma("sync", dst_hi, vsrc_hi)
                p.dma("sync", dst_lo, vsrc_lo)

        p.wait_now("vector", "tS")
        on1 = 0
        for reg in range(2):
            for h2 in range(2):
                for cbase, val in ((h2 * 2112 + 64, 1.0), (h2 * 2112 + 1056 + 64, 0.0)):
                    if reg == 0:
                        tgt = _setap(CKV.ap(), [[8192, 128], [66, NKT], [1, 1]], offset=cbase)
                    else:
                        tgt = _setap(PA.ap(), [[32768, 128], [66, NKT], [1, 1]], offset=VREG1 + cbase)
                    on1 = p.op("vector", "memset", tgt, val)
        p.wait("tensor", "vS", on1)
        vtok = {}
        for hp in range(NCH):
            p.qbarrier("sync", "gpsimd")
            p.wait_now("sync", "tS")
            p.dma("sync", pa(0, 2048), s_khi[hp * 128:(hp + 1) * 128, :])
            p.dma("sync", pa(2048, 2048), s_klo[hp * 128:(hp + 1) * 128, :])
            p.dma("sync", pa(4096, 2048), s_qhi[hp * 128:(hp + 1) * 128, :])
            p.dma("sync", pa(6144, 2048), s_qlo[hp * 128:(hp + 1) * 128, :])
            if hp == 0:
                v_load(0, 0)
            p.qwait("tensor", "sync")
            if hp > 0:
                p.qwait_tok("tensor", vtok[hp])
            if hp + 1 < NCH:
                v_load(hp + 1, (hp + 1) & 1)
                vtok[hp + 1] = p.qclose("sync")
            if layer == 1:
                for h2 in range(2):
                    h = 2 * hp + h2
                    p.wait_now("gpsimd", "tS")
                    gsrc = _setap(GSH1.ap(), [[4224, 1], [0, 128], [1, 4224]],
                                  offset=h * 4224)
                    gdst = _setap(s_gsh[h2], [[4224, 128], [1, 4224]])
                    p.dma("gpsimd", gdst, gsrc)
                    p.qwait("gpsimd", "gpsimd")
                    rsrc = _setap(s_gsh[h2], [[4223, 128], [1, 4223]])
                    p.dma("gpsimd", gsh_ap(h2, 0, 4223), rsrc)
                p.qwait("tensor", "gpsimd")
                mxv = max([0] + [v for v in ep_ctx.values()])
                mx_done = {}
                for qt in range(NKT):
                    for h2 in range(2):
                        base = 64 * h2
                        for ks in range(NTC):
                            bmx = bank(4 + h2 * 2 + (ks & 1))
                            p.wait("tensor", "vS", mx_done.get((h2, ks & 1), mxv))
                            qst = pa(4096 + qt * 128, 128, base, base + 64)
                            kmv = pa(0 + ks * 512, 512, base, base + 64)
                            mmx = mm(bmx, qst, kmv, start=True, stop=True)
                            p.wait("vector", "tS", mmx)
                            p.wait_now("vector", "vS")
                            col = 28 + h2 * 16 + qt
                            if ks == 0:
                                mxc = p.op("vector", "tensor_reduce",
                                           MISC.ap()[:, col:col + 1], bmx,
                                           axis=AX.X, op=AluOpType.max)
                            else:
                                p.op("vector", "tensor_reduce",
                                     MISC.ap()[:, 60 + h2:61 + h2], bmx, axis=AX.X,
                                     op=AluOpType.max)
                                p.wait_now("vector", "vS")
                                mxc = p.op("vector", "tensor_tensor",
                                           MISC.ap()[:, col:col + 1],
                                           MISC.ap()[:, col:col + 1],
                                           MISC.ap()[:, 60 + h2:61 + h2], op=AluOpType.max)
                            mx_done[(h2, ks & 1)] = mxc
                p.wait_now("tensor", "vS")
                trc = 0
                for h2 in range(2):
                    trc = p.op("tensor", "transpose", bank(4 + h2 * 2, 0, 16, 128),
                               MISC.ap()[:, 28 + h2 * 16:44 + h2 * 16], IDF.ap())
                p.wait("vector", "tS", trc)
                for h2 in range(2):
                    mrow = 32 if h2 == 0 else 64
                    p.wait_now("vector", "vS")
                    p.op("vector", "tensor_copy", MISC.ap()[mrow:mrow + 16, 0:128],
                         bank(4 + h2 * 2, 0, 16, 128))
                crw = 0
                for h2 in range(2):
                    mrow = 32 if h2 == 0 else 64
                    p.wait_now("gpsimd", "vS")
                    fsrc = _setap(MISC.ap(), [[2048, 16], [1, 128]], offset=mrow * 2048)
                    fdst = _setap(MISC.ap(), [[2048, 1], [1, 2048]], offset=96 * 2048)
                    p.dma("gpsimd", fdst, fsrc)
                    p.qwait("vector", "gpsimd")
                    crow_dst = (WL if h2 == 0 else WH).ap()[0:1, 0:2048]
                    crw = p.op("vector", "tensor_scalar", crow_dst,
                               MISC.ap()[96:97, 0:2048],
                               -1.0, EXPSHIFT, op0=AluOpType.mult, op1=AluOpType.add)
                p.wait("tensor", "vS", crw)
            for qc in range(NTC):
                kts = range(4 * qc + 4) if layer == 0 else range(NKT)
                nk_ = len(kts)
                first_av = {0: True, 1: True}
                ep_slot = {}

                def emit_av(i, kt):
                    par = i % 4
                    for h2 in range(2):
                        cslot = 4 + 2 * h2 + (qc & 1)
                        bC = bank(cslot, 0, 65)
                        es = ep_slot[(h2, par)]
                        p.wait("tensor", "sS", es[0])
                        p.wait("tensor", "vS", es[1])
                        if first_av[h2]:
                            p.wait("tensor", "vS", ep_ctx.get((h2, qc & 1), 0))
                        vhi_st = vav_st(hp & 1, h2 * 2112 + kt * 66, 65)
                        vlo_st = vav_st(hp & 1, h2 * 2112 + 1056 + kt * 66, 65)
                        mm(bC, vhi_st, phi_ap(h2, par), start=first_av[h2], stop=False)
                        mm(bC, vhi_st, plo_ap(h2, par), start=False, stop=False)
                        avdone[(h2, par)] = mm(bC, vlo_st, phi_ap(h2, par), start=False,
                                               stop=(i == nk_ - 1))
                        first_av[h2] = False

                pend = []
                for i, kt in enumerate(kts):
                    par = i % 4
                    for h2 in range(2):
                        base = 64 * h2
                        bS = h2 * 2 + (i & 1)
                        p.wait("tensor", "sS", ep_bank[bS][0])
                        p.wait("tensor", "vS", ep_bank[bS][1])
                        straddle = (layer == 0 and kt >= 4 * qc)
                        st0 = (not straddle) and (layer == 0)
                        if straddle:
                            mm(bank(bS), IDB.ap(),
                               CMSK.ap()[:, (kt - 4 * qc) * 512:(kt - 4 * qc + 1) * 512],
                               start=True, stop=False)
                        if layer == 1:
                            off = qc * 512 - kt * 128 + 2047
                            mm(bank(bS), IDB.ap(), gsh_ap(h2, off), start=True, stop=False)
                            crow = (WL if h2 == 0 else WH).ap()[0:1, qc * 512:(qc + 1) * 512]
                            mm(bank(bS), ONE1.ap()[0:1, 0:128], crow, start=False, stop=False)
                        kh = pa(0 + kt * 128, 128, base, base + 64)
                        kl = pa(2048 + kt * 128, 128, base, base + 64)
                        qh = pa(4096 + qc * 512, 512, base, base + 64)
                        ql = pa(6144 + qc * 512, 512, base, base + 64)
                        mm(bank(bS), kh, qh, start=st0, stop=False)
                        mm(bank(bS), kh, ql, start=False, stop=False)
                        mm(bank(bS), kl, qh, start=False, stop=True)
                    qk_done = p.cnt["tS"]
                    if len(pend) == 2:
                        emit_av(*pend.pop(0))
                    pend.append((i, kt))
                    for h2 in range(2):
                        bS = h2 * 2 + (i & 1)
                        p.wait("scalar", "tS", qk_done)
                        p.wait("scalar", "tS", avdone.get((h2, par), 0))
                        p32 = SC32.ap()[:, (h2 * 2 + (i & 1)) * 512:(h2 * 2 + (i & 1)) * 512 + 512]
                        p.wait_now("scalar", "sS")
                        p.wait("scalar", "vS", ep_p32.get((h2, i & 1), 0))
                        se = p.op("scalar", "activation", p32, bank(bS), AF.Exp,
                                  scale=0.125)
                        phi = phi_ap(h2, par)
                        plo = plo_ap(h2, par)
                        p.wait("scalar", "sS", se)
                        sh = p.op("scalar", "copy", phi, p32)
                        p.wait("vector", "sS", sh)
                        p.wait("vector", "tS", avdone.get((h2, par), 0))
                        vl = p.op("vector", "scalar_tensor_tensor", plo, p32, 1.0,
                                  phi, op0=AluOpType.mult, op1=AluOpType.subtract)
                        ep_bank[bS] = (sh, vl)
                        ep_slot[(h2, par)] = (sh, vl)
                        ep_p32[(h2, i & 1)] = vl
                for pe_ in pend:
                    emit_av(*pe_)
                for h2 in range(2):
                    h = 2 * hp + h2
                    cslot = 4 + 2 * h2 + (qc & 1)
                    p.wait("vector", "tS", avdone[(h2, (nk_ - 1) % 4)])
                    p.wait_now("vector", "vS")
                    p.wait_now("vector", "sS")
                    rz = MISC.ap()[0:1, 1536:2048]
                    p.op("vector", "reciprocal", rz, bank(cslot, 64, 65))
                    p.wait_now("gpsimd", "vS")
                    zb = MISC.ap()[0:64, 512:1024]
                    bcast_row(zb, rz, 512)
                    p.qwait("vector", "gpsimd")
                    ctx32 = MISC.ap()[0:64, 1024:1536]
                    vc3 = p.op("vector", "tensor_tensor", ctx32, bank(cslot, 0, 64),
                               zb, op=AluOpType.mult)
                    ep_ctx[(h2, qc & 1)] = vc3
                    p.wait("scalar", "vS", vc3)
                    spc.spill(ctx32,
                              s_ctxhi[h * 64:(h + 1) * 64, qc * 512:(qc + 1) * 512],
                              s_ctxlo[h * 64:(h + 1) * 64, qc * 512:(qc + 1) * 512])
                if pump_sched:
                    pump(pump_sched[qc])


    convert_fp16(h1t, s_h1, DH, DH)
    convert_fp16(h2t, s_h2, DH, DH)
    convert_fp16(w3t, s_w3, DH, D)
    mark("attn0")
    attn_core(0, pump_sched=(0, 3, 3, 2))
    if "ctx0" in dbg:
        p.qbarrier("sync", "gpsimd")
        p.dma("sync", dbg_t["ctxhi"][:, :], s_ctxhi[:, :])
        p.dma("sync", dbg_t["ctxlo"][:, :], s_ctxlo[:, :])

    def out_proj(wsrc):
        p.wait_now("sync", "tS")
        p.qbarrier("sync", "gpsimd")
        for c in range(NCH):
            p.dma("sync", pa(c * 2048, 2048), s_ctxhi[c * 128:(c + 1) * 128, :])
            p.dma("sync", pa(16384 + c * 2048, 2048), s_ctxlo[c * 128:(c + 1) * 128, :])
        p.qwait("tensor", "sync")
        stg = Stager()
        epo = {0: 0, 1: 0}
        for oc in range(NCH):
            slot = stg.stage(wsrc, D, oc * 128, 128)
            bpar = oc & 1
            p.wait("tensor", "vS", epo[bpar])
            mmc = 0
            for pas in range(3):
                for kc in range(NCH):
                    st = (WH if pas < 2 else WL).ap()[:, slot + kc * 128:slot + kc * 128 + 128]
                    for tc in range(NTC):
                        mv = pa(kc * 2048 + tc * 512, 512) if pas != 1 else pa(16384 + kc * 2048 + tc * 512, 512)
                        mmc = mm(bank(bpar * 4 + tc), st, mv, start=(pas == 0 and kc == 0),
                                 stop=(pas == 2 and kc == NCH - 1))
            stg.done(mmc)
            p.wait("vector", "tS", mmc)
            for tc in range(NTC):
                epo[bpar] = p.op("vector", "scalar_tensor_tensor", X[oc].ap()[:, tc * 512:(tc + 1) * 512],
                                 bank(bpar * 4 + tc), 1.0, X[oc].ap()[:, tc * 512:(tc + 1) * 512],
                                 op0=AluOpType.mult, op1=AluOpType.add)

    mark("wo")
    out_proj(w_o)
    if "x1a" in dbg:
        p.wait_now("gpsimd", "vS")
        for c in range(NCH):
            p.dma("gpsimd", dbg_t["x1a"][c * 128:(c + 1) * 128, :], X[c].ap())
    if upto <= 4:
        p.finalize()
        return nc, di, dbg_t

    mark("convert")
    pump(max(0, 64 - conv_done[0]))

    def hstage(src16, movh_base, movl_base, dsthi, dstlo, scol, load_src_hi, load_src_lo):
        ep2 = {i: (0, 0) for i in range(4)}
        sph = SpillSlots(WL, WL, [(0, 1024), (512, 1536)])
        hst_tdone = [p.cnt["tS"], p.cnt["tS"]]
        t32v = [p.cnt["vS"], p.cnt["vS"]]
        for tc in range(NTC):
            p.wait_now("sync", "tS")
            p.wait_now("sync", "sS")
            p.wait_now("sync", "vS")
            p.qbarrier("sync", "gpsimd")
            src_hi = _setap(load_src_hi, [[T, 128], [128 * T, 32], [1, 512]],
                            offset=tc * 512)
            src_lo = _setap(load_src_lo, [[T, 128], [128 * T, 32], [1, 512]],
                            offset=tc * 512)
            dst_hi = _setap(PA.ap(), [[32768, 128], [512, 32], [1, 512]], offset=0)
            dst_lo = _setap(PA.ap(), [[32768, 128], [512, 32], [1, 512]], offset=16384)
            p.dma("sync", dst_hi, src_hi)
            p.dma("sync", dst_lo, src_lo)
            p.qwait("tensor", "sync")
            for oc in range(32):
                if oc & 1:
                    pump(1, drain=False)
                bk = bank(oc & 3)
                p.wait("tensor", "sS", ep2[oc & 3][0])
                p.wait("tensor", "vS", ep2[oc & 3][1])
                for khalf in range(2):
                    par = khalf
                    p.wait("sync", "tS", hst_tdone[par])
                    src = _setap(src16, [[src16.shape[1], 128], [128 * src16.shape[1], 16], [1, 128]],
                                 offset=khalf * 2048 * src16.shape[1] + oc * 128)
                    p.dma("sync", WH.ap()[:, par * 2048:par * 2048 + 2048], src)
                    p.qwait("tensor", "sync")
                    hst_tdone[par] = p.cnt["tS"] + 32
                    for pas in range(2):
                        for k2 in range(16):
                            kc = khalf * 16 + k2
                            mv = pa((0 if pas == 0 else 16384) + kc * 512, 512)
                            mm(bk, WH.ap()[:, par * 2048 + k2 * 128:par * 2048 + k2 * 128 + 128],
                               mv, start=(khalf == 0 and pas == 0 and k2 == 0),
                               stop=(khalf == 1 and pas == 1 and k2 == 15))
                mmc = p.cnt["tS"]
                p.wait("scalar", "tS", mmc)
                p.wait_now("scalar", "sS")
                p.wait("scalar", "vS", t32v[oc & 1])
                t32 = SC32.ap()[:, (oc & 1) * 512:(oc & 1) * 512 + 512]
                p.op("scalar", "activation", t32, bk, AF.Gelu,
                     scale=SVEC.ap()[:, scol + oc:scol + oc + 1])
                ep2[oc & 3] = sph.spill(t32,
                                        dsthi[oc * 128:(oc + 1) * 128, tc * 512:(tc + 1) * 512],
                                        dstlo[oc * 128:(oc + 1) * 128, tc * 512:(tc + 1) * 512])
                t32v[oc & 1] = ep2[oc & 3][1]

    mark("s2")
    hstage(s_h1, 0, 16384, s_bhi, s_blo, 64, s_ahi, s_alo)
    mark("s3")
    hstage(s_h2, 0, 16384, s_ahi, s_alo, 96, s_bhi, s_blo)

    pump_all()
    p.wait_now("vector", "sS")
    p.wait_now("gpsimd", "sS")
    mark("s4")
    ep_v4 = 0
    s4_tdone = [p.cnt["tS"], p.cnt["tS"]]
    for tc in range(NTC):
        p.wait_now("sync", "tS")
        p.wait_now("sync", "vS")
        p.qbarrier("sync", "gpsimd")
        src_hi = _setap(s_ahi, [[T, 128], [128 * T, 32], [1, 512]], offset=tc * 512)
        src_lo = _setap(s_alo, [[T, 128], [128 * T, 32], [1, 512]], offset=tc * 512)
        dst_hi = _setap(PA.ap(), [[32768, 128], [512, 32], [1, 512]], offset=0)
        dst_lo = _setap(PA.ap(), [[32768, 128], [512, 32], [1, 512]], offset=16384)
        p.dma("sync", dst_hi, src_hi)
        p.dma("sync", dst_lo, src_lo)
        p.qwait("tensor", "sync")
        for oc in range(NCH):
            bk = bank(oc & 3)
            p.wait("tensor", "vS", ep_v4)
            mmc = 0
            for khalf in range(2):
                p.wait("sync", "tS", s4_tdone[khalf])
                src = _setap(s_w3, [[D, 128], [128 * D, 16], [1, 128]],
                             offset=khalf * 2048 * D + oc * 128)
                p.dma("sync", WH.ap()[:, khalf * 2048:khalf * 2048 + 2048], src)
                p.qwait("tensor", "sync")
                for pas in range(2):
                    for k2 in range(16):
                        kc = khalf * 16 + k2
                        mv = pa((0 if pas == 0 else 16384) + kc * 512, 512)
                        mmc = mm(bk, WH.ap()[:, khalf * 2048 + k2 * 128:khalf * 2048 + k2 * 128 + 128],
                                 mv, start=(khalf == 0 and pas == 0 and k2 == 0),
                                 stop=(khalf == 1 and pas == 1 and k2 == 15))
                s4_tdone[khalf] = mmc
            p.wait("vector", "tS", mmc)
            ep_v4 = p.op("vector", "scalar_tensor_tensor",
                         X[oc].ap()[:, tc * 512:(tc + 1) * 512], bk,
                         SVEC.ap()[:, 128 + oc:128 + oc + 1],
                         X[oc].ap()[:, tc * 512:(tc + 1) * 512],
                         op0=AluOpType.mult, op1=AluOpType.add)

    if "x1" in dbg:
        p.wait_now("gpsimd", "vS")
        for c in range(NCH):
            p.dma("gpsimd", dbg_t["x1"][c * 128:(c + 1) * 128, :], X[c].ap())
    if upto <= 5:
        p.finalize()
        return nc, di, dbg_t

    mark("pos")
    p.wait_now("sync", "vS")
    p.wait_now("sync", "sS")
    p.dma("sync", SC32.ap()[0:101, 0:2048], protod_in[:, :])
    p.dma("sync", MISC.ap()[0:101, 0:4], coeffsT[:, :])
    p.dma("sync", MISC.ap()[0:4, 16:32], wposT[:, :])
    p.dma("sync", CMSK.ap()[0:1, 0:2048], tie_in[:, :])
    p.qwait("tensor", "sync")
    mmp = 0
    for tc in range(NTC):
        mmp = mm(bank(tc, 0, 4), MISC.ap()[0:101, 0:4], SC32.ap()[0:101, tc * 512:(tc + 1) * 512],
                 start=True, stop=False)
    p.wait("sync", "tS", mmp)
    p.dma("sync", SC32.ap()[0:101, 0:2048], protod_lo_in[:, :])
    p.qwait("tensor", "sync")
    for tc in range(NTC):
        mm(bank(tc, 0, 4), MISC.ap()[0:101, 0:4], SC32.ap()[0:101, tc * 512:(tc + 1) * 512],
           start=False, stop=True)
    p.wait_now("vector", "tS")
    vtd = 0
    for tc in range(NTC):
        vtd = p.op("vector", "tensor_copy", SC32.ap()[0:4, tc * 512:(tc + 1) * 512],
                   bank(tc, 0, 4))
    p.wait("tensor", "vS", vtd)
    for tc in range(NTC):
        mm(bank(4 + tc, 0, 16), MISC.ap()[0:4, 16:32],
           SC32.ap()[0:4, tc * 512:(tc + 1) * 512], start=True, stop=True)
    p.wait_now("scalar", "tS")
    sg8 = 0
    for tc in range(NTC):
        sg8 = p.op("scalar", "activation", MISC.ap()[64:80, tc * 512:(tc + 1) * 512],
                   bank(4 + tc, 0, 16), AF.Copy, scale=8.0)
    p.wait("vector", "sS", sg8)
    gl = p.op("vector", "tensor_copy",
              GSH1.ap()[0:16, 0:2047],
              _setap(MISC.ap(), [[2048, 16], [-1, 2047]], offset=64 * 2048 + 2047))
    p.wait_now("vector", "vS")
    gr = p.op("vector", "tensor_copy", GSH1.ap()[0:16, 2047:4095],
              MISC.ap()[64:80, 0:2048])
    gz = p.op("vector", "memset", GSH1.ap()[0:16, 4095:4224], 0.0)
    if "gsh" in dbg:
        p.wait_now("gpsimd", "vS")
        p.dma("gpsimd", dbg_t["gsh"][:, :], GSH1.ap()[0:16, 0:4224])

    p.wait_now("scalar", "tS")
    p.wait_now("vector", "tS")
    mark("ln1")
    ln_phase("ln1")
    if "xn1" in dbg:
        p.wait_now("gpsimd", "vS")
        p.wait_now("gpsimd", "sS")
        for c in range(NCH):
            p.dma("gpsimd", dbg_t["xn1"][c * 128:(c + 1) * 128, 0:T], xn_hi(c))
            p.dma("gpsimd", dbg_t["xn1"][c * 128:(c + 1) * 128, T:2 * T], xn_lo(c))

    mark("qkv")
    def qkv_route(oc):
        if oc < 8:
            return s_qhi, s_qlo, oc * 128
        return s_khi, s_klo, (oc - 8) * 128

    proj_general(wqkvt, 16 * 128, xn_hi, xn_lo, NCH, None, qkv_route=qkv_route)

    mark("l1v")
    for dvc in range(2):
        sc1, vc1 = stage_general(wqkvt, 512, 2048 + dvc * 512, 512, slot=0, row0=0)
        sc2, vc2 = stage_general(wqkvt, 512, 2048 + dvc * 512, 512, slot=2048, row0=512)
        p.wait("tensor", "sS", sc2)
        p.wait("tensor", "vS", vc2)
        if dvc == 0 and "vslab" in dbg:
            p.wait_now("gpsimd", "sS")
            p.wait_now("gpsimd", "vS")
            p.dma("gpsimd", dbg_t["vslabh"][:, :], WH.ap())
            p.dma("gpsimd", dbg_t["vslabl"][:, :], WL.ap())
            p.qwait("tensor", "gpsimd")
        spv1 = SpillSlots(CKV, CKV, [(4096, 4608), (5120, 5632), (6144, 6656), (7168, 7680)])
        epv1 = {k: (0, 0) for k in range(4)}
        for tt in range(NKT):
            bpar = tt & 3
            bk = bank(bpar)
            p.wait("tensor", "sS", epv1[bpar][0])
            p.wait("tensor", "vS", epv1[bpar][1])
            mmc = 0
            for pas in range(3):
                for kc in range(NCH):
                    slot = 0 if kc < 4 else 2048
                    k2 = kc % 4
                    wsl = (WH if pas < 2 else WL).ap()[:, slot + k2 * 512: slot + k2 * 512 + 512]
                    stt = (xn_hi(kc) if pas != 1 else xn_lo(kc))[:, tt * 128:(tt + 1) * 128]
                    mmc = mm(bk, stt, wsl, start=(pas == 0 and kc == 0),
                             stop=(pas == 2 and kc == NCH - 1))
            p.wait("scalar", "tS", mmc)
            p.wait("vector", "tS", mmc)
            epv1[bpar] = spv1.spill(bk,
                                    s_vhi[tt * 128:(tt + 1) * 128, dvc * 512:(dvc + 1) * 512],
                                    s_vlo[tt * 128:(tt + 1) * 128, dvc * 512:(dvc + 1) * 512])

    if "kvq1" in dbg:
        p.qbarrier("sync", "gpsimd")
        p.dma("sync", dbg_t["khi"][:, :], s_khi[:, :])
        p.dma("sync", dbg_t["klo"][:, :], s_klo[:, :])
        p.dma("sync", dbg_t["qhi"][:, :], s_qhi[:, :])
        p.dma("sync", dbg_t["qlo"][:, :], s_qlo[:, :])
        p.dma("sync", dbg_t["vhi"][:, :], s_vhi[:, :])
        p.dma("sync", dbg_t["vlo"][:, :], s_vlo[:, :])
    if upto <= 6:
        p.finalize()
        return nc, di, dbg_t

    mark("logits")
    sc, vc = stage_general(ctrl, D, 0, 64, slot=0)
    p.wait("tensor", "sS", sc)
    p.wait("tensor", "vS", vc)
    p.wait_now("tensor", "vS")
    mmc = 0
    for tc in range(NTC):
        mm(bank(tc, 0, 64), ONE1.ap()[0:1, 0:64],
           CMSK.ap()[0:1, tc * 512:(tc + 1) * 512], start=True, stop=False)
        for pas in range(3):
            for kc in range(NCH):
                st = (WH if pas < 2 else WL).ap()[:, kc * 64:(kc + 1) * 64]
                mv = xn_hi(kc, tc) if pas != 1 else xn_lo(kc, tc)
                mmc = mm(bank(tc, 0, 64), st, mv, start=False,
                         stop=(pas == 2 and kc == NCH - 1))
    p.wait("vector", "tS", mmc)
    vlt = 0
    for tc in range(NTC):
        vlt = p.op("vector", "tensor_copy", SC32.ap()[0:64, tc * 512:(tc + 1) * 512],
                   bank(tc, 0, 64))
    p.wait("vector", "vS", vlt)
    gmx = p.op("vector", "tensor_reduce", MISC.ap()[0:64, 0:128],
               SC32.ap()[0:64, :].rearrange("p (g t) -> p g t", t=16),
               axis=AX.X, op=AluOpType.max)
    p.wait("vector", "vS", gmx)
    prm = p.op("vector", "tensor_tensor",
               CKV.ap()[0:64, 0:2048].rearrange("p (g t) -> p g t", t=16),
               SC32.ap()[0:64, :].rearrange("p (g t) -> p g t", t=16),
               MISC.ap()[0:64, 0:128].broadcast_to([64, 128, 16]),
               op=AluOpType.is_equal)
    if "perm" in dbg:
        p.wait("gpsimd", "vS", prm)
        p.dma("gpsimd", dbg_t["perm"][:, :], CKV.ap()[0:64, 0:2048])
        p.dma("gpsimd", dbg_t["lg"][:, :], SC32.ap()[0:64, 0:2048])

    mark("y")
    ep_y = {k: (0, 0) for k in range(8)}
    stgy = Stager()
    spy = SpillSlots(WH, WL, [(2048, 2048), (2560, 2560), (3072, 3072), (3584, 3584)])
    vm_last = prm
    for oc in range(32):
        slot = stgy.stage(f1m, D, oc * 128, 128)
        p.wait("gpsimd", "vS", vm_last)
        psrc = _setap(CKV.ap(), [[8192, 2], [0, 64], [1, 2048]], offset=2 * oc * 8192)
        p.dma("gpsimd", ckv_ap(2048, 2048), psrc)
        p.qwait("tensor", "gpsimd")
        for tc in range(NTC):
            bidx = (oc & 1) * 4 + tc
            bk = bank(bidx)
            p.wait("tensor", "sS", ep_y[bidx][0])
            p.wait("tensor", "vS", ep_y[bidx][1])
            mmc = 0
            for pas in range(3):
                for kc in range(NCH):
                    st = (WH if pas < 2 else WL).ap()[:, slot + kc * 128:slot + kc * 128 + 128]
                    mv = xn_hi(kc, tc) if pas != 1 else xn_lo(kc, tc)
                    mmc = mm(bk, st, mv, start=(pas == 0 and kc == 0),
                             stop=(pas == 2 and kc == NCH - 1))
            if tc == NTC - 1:
                stgy.done(mmc)
            p.wait("scalar", "tS", mmc)
            p.wait_now("scalar", "sS")
            t32 = SC32.ap()[:, (tc & 1) * 512:(tc & 1) * 512 + 512]
            sr = p.op("scalar", "activation", t32, bk, AF.Relu,
                      bias=SVEC.ap()[:, 136 + oc:136 + oc + 1], scale=1.0)
            p.wait("vector", "sS", sr)
            p.wait_now("vector", "vS")
            m32 = MISC.ap()[:, (tc & 1) * 512:(tc & 1) * 512 + 512]
            vm = p.op("vector", "tensor_tensor", m32, t32,
                      ckv_ap(2048 + tc * 512, 512), op=AluOpType.mult)
            p.wait("scalar", "vS", vm)
            vm_last = vm
            ep_y[bidx] = spy.spill(m32,
                                   s_ahi[oc * 128:(oc + 1) * 128, tc * 512:(tc + 1) * 512],
                                   s_alo[oc * 128:(oc + 1) * 128, tc * 512:(tc + 1) * 512])
    if "ym" in dbg:
        p.qbarrier("sync", "gpsimd")
        p.dma("sync", dbg_t["ymhi"][:, :], s_ahi[:, :])
        p.dma("sync", dbg_t["ymlo"][:, :], s_alo[:, :])
    if upto <= 7:
        p.finalize()
        return nc, di, dbg_t

    convert_split_fp16(f2m, s_f2hi, s_f2lo, DH, D)
    mark("attn1")
    attn_core(1, pump_sched=(0, 2, 1, 1))
    mark("wout")
    out_proj(woutt)
    if "x2a" in dbg:
        p.wait_now("gpsimd", "vS")
        for c in range(NCH):
            p.dma("gpsimd", dbg_t["x2a"][c * 128:(c + 1) * 128, :], X[c].ap())
    if upto <= 8:
        p.finalize()
        return nc, di, dbg_t


    mark("f2conv")
    pump_all()
    mark("f2")
    ep_v5 = 0
    f2_tdone = [p.cnt["tS"], p.cnt["tS"]]
    for tc in range(NTC):
        p.wait_now("sync", "tS")
        p.wait_now("sync", "vS")
        p.qbarrier("sync", "gpsimd")
        src_hi = _setap(s_ahi, [[T, 128], [128 * T, 32], [1, 512]], offset=tc * 512)
        src_lo = _setap(s_alo, [[T, 128], [128 * T, 32], [1, 512]], offset=tc * 512)
        dst_hi = _setap(PA.ap(), [[32768, 128], [512, 32], [1, 512]], offset=0)
        dst_lo = _setap(PA.ap(), [[32768, 128], [512, 32], [1, 512]], offset=16384)
        p.dma("sync", dst_hi, src_hi)
        p.dma("sync", dst_lo, src_lo)
        p.qwait("tensor", "sync")
        for oc in range(NCH):
            bk = bank(oc & 3)
            p.wait("tensor", "vS", ep_v5)
            mmc = 0
            for khalf in range(2):
                p.wait("sync", "tS", f2_tdone[khalf])
                srch = _setap(s_f2hi, [[D, 128], [128 * D, 16], [1, 128]],
                              offset=khalf * 2048 * D + oc * 128)
                srcl = _setap(s_f2lo, [[D, 128], [128 * D, 16], [1, 128]],
                              offset=khalf * 2048 * D + oc * 128)
                p.dma("sync", WH.ap()[:, khalf * 2048:khalf * 2048 + 2048], srch)
                p.dma("sync", WL.ap()[:, khalf * 2048:khalf * 2048 + 2048], srcl)
                p.qwait("tensor", "sync")
                for pas in range(3):
                    wsl = (WH if pas < 2 else WL)
                    for k2 in range(16):
                        kc = khalf * 16 + k2
                        mv = pa((0 if pas != 1 else 16384) + kc * 512, 512)
                        mmc = mm(bk, wsl.ap()[:, khalf * 2048 + k2 * 128:khalf * 2048 + k2 * 128 + 128],
                                 mv, start=(khalf == 0 and pas == 0 and k2 == 0),
                                 stop=(khalf == 1 and pas == 2 and k2 == 15))
                f2_tdone[khalf] = mmc
            p.wait("vector", "tS", mmc)
            ep_v5 = p.op("vector", "scalar_tensor_tensor",
                         X[oc].ap()[:, tc * 512:(tc + 1) * 512], bk, 1.0,
                         X[oc].ap()[:, tc * 512:(tc + 1) * 512],
                         op0=AluOpType.mult, op1=AluOpType.add)

    mark("outp")
    evac_prev2 = [0, 0]
    for t in range(NKT):
        par = t & 1
        p.wait("tensor", "vS", evac_prev2[par])
        if t == 0:
            p.wait_now("tensor", "vS")
        tcnt = 0
        for c in range(NCH):
            tcnt = p.op("tensor", "transpose",
                        PS.ap()[:, par * 1024 + c * 128:par * 1024 + c * 128 + 128],
                        X[c].ap()[:, t * 128:(t + 1) * 128], IDF.ap())
        p.wait("vector", "tS", tcnt)
        if t >= 2:
            p.qwait("vector", "gpsimd")
        ev = p.op("vector", "tensor_copy",
                  SC32.ap()[:, par * 1024:par * 1024 + 1024],
                  PS.ap()[:, par * 1024:par * 1024 + 1024])
        evac_prev2[par] = ev
        p.wait("gpsimd", "vS", ev)
        p.dma("gpsimd", out_hbm[t * 128:(t + 1) * 128, :],
              SC32.ap()[:, par * 1024:par * 1024 + 1024])

    p.finalize()
    return nc, di, dbg_t


def _build_consts():
    identf = np.eye(128, dtype=np.float32)
    identb = np.eye(128, dtype=ml_dtypes.bfloat16)
    onesb = np.ones((1, 128), dtype=ml_dtypes.bfloat16)
    j = np.arange(512)[None, :]
    pp = np.arange(128)[:, None]
    cm = np.zeros((128, 4 * 512), dtype=np.float32)
    for d in range(4):
        cm[:, d * 512:(d + 1) * 512] = np.where(pp + d * 128 > j, NEG, 0.0)
    cmask = cm.astype(ml_dtypes.bfloat16)
    proto = np.linspace(0.0, 1.0, 512).astype(np.float32)
    rel = (np.arange(T).astype(np.float32) / np.float32(T - 1))
    idx = np.clip(np.round(rel * np.float32(511)).astype(np.int32), 0, 511)
    KG = np.arange(101)[:, None]
    IDXg = idx[None, :]
    sel = IDXg - KG
    proto64 = np.linspace(0.0, 1.0, 512)
    pd64 = np.where(sel >= 0, proto64[np.clip(sel, 0, 511)], 0.0)
    protod = pd64.astype(np.float32)
    protod_lo = (pd64 - protod.astype(np.float64)).astype(np.float32)
    tie = (np.linspace(0.0, 1e-6, 16)[np.arange(T) % 16]).astype(ml_dtypes.bfloat16)[None, :]
    return dict(identf=identf, identb=identb, onesb=onesb, cmask=cmask,
                protod=protod, protod_lo=protod_lo, tie=tie)


def _prep_core(inputs, b, consts):
    m = dict(consts)
    m["x"] = np.ascontiguousarray(inputs["L0_attn_x"][b])
    m["w_dkv"] = inputs["L0_attn_W_DKV"]
    m["w_uk"] = inputs["L0_attn_W_UK"]
    m["w_uv"] = inputs["L0_attn_W_UV"]
    m["w_uq"] = inputs["L0_attn_W_UQ"]
    m["w_o"] = inputs["L0_attn_W_O"]
    m["w1t"] = np.ascontiguousarray(inputs["L0_ffn_w1_w"].T)
    m["w2t"] = np.ascontiguousarray(inputs["L0_ffn_w2_w"].T)
    m["h1t"] = np.ascontiguousarray(inputs["L0_ffn_h1_w"].T)
    m["h2t"] = np.ascontiguousarray(inputs["L0_ffn_h2_w"].T)
    m["w3t"] = np.ascontiguousarray(inputs["L0_ffn_w3_w"].T)
    parts = [inputs["L0_ffn_w1_s"], inputs["L0_ffn_w2_s"], inputs["L0_ffn_h1_s"],
             inputs["L0_ffn_h2_s"], inputs["L0_ffn_w3_s"],
             inputs["L1_ffn_bias"].reshape(-1)]
    flat = np.concatenate([np.asarray(x, np.float32).reshape(-1) for x in parts])
    flat = np.concatenate([flat, np.zeros(192 * 128 - flat.size, np.float32)])
    m["svec"] = np.ascontiguousarray(flat.reshape(192, 128).T)
    m["wqkvt"] = np.ascontiguousarray(inputs["L1_attn_Wqkv"].T)
    m["woutt"] = np.ascontiguousarray(inputs["L1_attn_Wout"].T)
    m["wposT"] = np.ascontiguousarray(inputs["L1_attn_Wpos"].T)
    m["coeffsT"] = np.ascontiguousarray(inputs["L1_attn_coeffs"].T)
    m["ctrl"] = np.ascontiguousarray(inputs["L1_ffn_controller"].reshape(D, 64))
    m["f1m"] = np.ascontiguousarray(inputs["L1_ffn_f1"].reshape(D, DH))
    m["f2m"] = np.ascontiguousarray(inputs["L1_ffn_f2"].reshape(DH, D))
    return m


def get_phases(nc_tuple=None):
    return None


def kernel(**inputs):
    consts = _build_consts()
    nc, di, dbg_t = build_nc()
    in_maps = [_prep_core(inputs, b, consts) for b in range(8)]
    res = run_bass_kernel_spmd(nc, in_maps, core_ids=list(range(8)))
    return np.stack([np.asarray(r["out"], np.float32) for r in res.results], axis=0)
